# version 02
# baseline (speedup 1.0000x reference)
import numpy as np
import ml_dtypes
import concourse.bass as bass
import concourse.mybir as mybir

F32 = mybir.dt.float32
BF16 = mybir.dt.bfloat16
AF = mybir.ActivationFunctionType
ALU = mybir.AluOpType
AX = mybir.AxisListType

D = 1024
INNER = 2048
FF = 4096
KD = 8
KE = 16
NF = 32
T = 512
NSUB = 4
EPS = 1e-5
bfdt = ml_dtypes.bfloat16


class Prog:
    def __init__(self):
        self.ops = []
        self.cnt = {}
        self.serial = False

    def c(self, sem):
        return self.cnt.get(sem, 0)

    def op(self, eng, fn, waits=(), inc=None, n=1):
        ws = [(s, v) for (s, v) in waits if v and v > 0]
        if self.serial:
            ws = ws + [(s, v) for s, v in self.cnt.items() if v > 0]
        self.ops.append((eng, ws, fn, inc, n))
        if inc is not None:
            self.cnt[inc] = self.c(inc) + n
        return self.c(inc) if inc is not None else None


def carve(arena_ap, off, shape, dtype):
    n = 1
    for s in shape:
        n *= s
    if dtype == F32:
        sl = arena_ap[:, off:off + 2 * n].bitcast(F32)
    else:
        assert dtype == BF16
        sl = arena_ap[:, off:off + n]
    if len(shape) == 1:
        return sl
    pat = " ".join(f"a{i}" for i in range(len(shape)))
    kw = {f"a{i}": shape[i] for i in range(len(shape))}
    return sl.rearrange(f"p ({pat}) -> p {pat}", **kw)


def build_program(S, debug=False, serial_b=False, serial_loads=False):
    NCH = S // T
    nc = bass.Bass("TRN2", target_bir_lowering=False)
    sckind = "ExternalOutput" if debug else "Internal"

    x_d = nc.dram_tensor("x", [S, D], F32, kind="ExternalInput").ap()
    win_d = nc.dram_tensor("Win", [D, FF], F32, kind="ExternalInput").ap()
    wd_d = nc.dram_tensor("Wd", [INNER, INNER], F32, kind="ExternalInput").ap()
    wo_d = nc.dram_tensor("Wo", [INNER, D], F32, kind="ExternalInput").ap()
    wft_d = nc.dram_tensor("WfT", [D, D], F32, kind="ExternalInput").ap()
    disha_d = nc.dram_tensor("disha", [128, D], F32, kind="ExternalInput").ap()
    gcol_d = nc.dram_tensor("gcol", [128, KD], F32, kind="ExternalInput").ap()
    bcol_d = nc.dram_tensor("bcol", [128, KD], F32, kind="ExternalInput").ap()
    bincol_d = nc.dram_tensor("bincol", [128, NF], F32, kind="ExternalInput").ap()
    bdcol_d = nc.dram_tensor("bdcol", [128, KE], F32, kind="ExternalInput").ap()
    bocol_d = nc.dram_tensor("bocol", [128, KD], F32, kind="ExternalInput").ap()
    cbcol_d = nc.dram_tensor("cbcol", [128, KE], F32, kind="ExternalInput").ap()
    cwcol_d = nc.dram_tensor("cwcol", [128, 3 * KE], F32, kind="ExternalInput").ap()
    eyef_d = nc.dram_tensor("eyef", [128, 128], F32, kind="ExternalInput").ap()
    eyeb_d = nc.dram_tensor("eyeb", [128, 128], BF16, kind="ExternalInput").ap()
    y_d = nc.dram_tensor("y", [S, D], F32, kind="ExternalOutput").ap()

    sc_a = nc.dram_tensor("sc_a", [INNER, S], BF16, kind=sckind).ap()
    sc_b = nc.dram_tensor("sc_b", [INNER, S], BF16, kind=sckind).ap()
    sc_f = nc.dram_tensor("sc_f", [INNER, S], BF16, kind=sckind).ap()
    sc_g = nc.dram_tensor("sc_g", [INNER, S], BF16, kind=sckind).ap()
    sc_win = nc.dram_tensor("sc_win", [128, NF, KD, 128], BF16, kind="Internal").ap()

    sc_a_v = sc_a.rearrange("(e p) s -> p e s", p=128)
    sc_b_v = sc_b.rearrange("(e p) s -> p e s", p=128)
    sc_f_v = sc_f.rearrange("(e p) s -> p e s", p=128)
    sc_g_v = sc_g.rearrange("(e p) s -> p e s", p=128)

    ctx_list = []

    def sb(name, shape, dtype):
        g = nc.sbuf_tensor("s_" + name, shape, dtype)
        h = g.__enter__()
        ctx_list.append(g)
        return h.ap()

    gcol = sb("gcol", [128, KD], F32)
    bcolb = sb("bcolb", [128, KD], BF16)
    bincol = sb("bincol", [128, NF], F32)
    bdcol = sb("bdcol", [128, KE], F32)
    nbdcol = sb("nbdcol", [128, KE], F32)
    bocol = sb("bocol", [128, KD], F32)
    cbcol = sb("cbcol", [128, KE], F32)
    cwcol = sb("cwcol", [128, 3 * KE], F32)
    cin = sb("cin", [128, NF], F32)
    eyef = sb("eyef", [128, 128], F32)
    eyeb = sb("eyeb", [128, 128], BF16)
    dishab = sb("dishab", [128, D], BF16)
    strs1 = sb("strs1", [128, NSUB * NCH], F32)
    stc1 = sb("stc1", [128, NSUB * NCH], F32)
    stA = sb("stA", [128, 64], F32)
    stB = sb("stB", [128, 64], F32)
    fwd_c = sb("fwd_c", [128, KE], F32)
    bwd_c = sb("bwd_c", [128, KE], F32)
    xsq = sb("xsq", [128, D], F32)

    AR = 98304
    arena = sb("arena", [128, AR], BF16)

    wd = carve(arena, 0, [KE, INNER], BF16)
    win_r = carve(arena, 32768, [4, KD, 128], BF16)
    xr = carve(arena, 36864, [4, D], F32)
    h_cm = carve(arena, 45056, [KD, T], BF16)
    vr = carve(arena, 49152, [KE, T + 4], BF16)
    v_bf = carve(arena, 57408, [KE, T], BF16)
    ar = carve(arena, 65600, [8, T], BF16)
    br = carve(arena, 69696, [8, T], BF16)
    fr = carve(arena, 73792, [8, T], BF16)
    sgr = carve(arena, 77888, [8, T], BF16)
    cvt = carve(arena, 81984, [2, 2, T], F32)
    ur = carve(arena, 88128, [2, D], BF16)
    sneg = carve(arena, 90176, [2, T], F32)
    wstage = carve(arena, 65600, [FF], BF16)

    wout = carve(arena, 0, [KE, D], BF16)
    weff = carve(arena, 16384, [KD, D], BF16)
    ain = carve(arena, 24576, [8, T], BF16)
    binb = carve(arena, 28672, [8, T], BF16)
    fwdin = carve(arena, 32768, [8, T], BF16)
    sgin = carve(arena, 36864, [8, T], BF16)
    fs_bf = carve(arena, 40960, [KE, T], BF16)
    xB = carve(arena, 49152, [NSUB, D], F32)
    yB = carve(arena, 57344, [NSUB, D], F32)
    x1 = carve(arena, 65536, [NSUB, D], F32)
    aftm = carve(arena, 73728, [NSUB, D], F32)
    ln1cm = carve(arena, 81920, [KD, T], BF16)
    bb = carve(arena, 86016, [4, T], BF16)
    lnyr = carve(arena, 88064, [2, D], F32)
    l1r = carve(arena, 92160, [2, D], BF16)
    aor = carve(arena, 94208, [2, T], F32)
    tbb = carve(arena, 96256, [T], BF16)

    def psb(name, shape, dtype):
        g = nc.psum_tensor(name, shape, dtype)
        h = g.__enter__()
        ctx_list.append(g)
        return h.ap()

    pA = psb("pA", [128, T], F32)
    pB = psb("pB", [128, T], F32)
    ptb = psb("ptb", [128, 1024], BF16)
    ptr = psb("ptr", [128, T], F32)
    pc32 = psb("pc32", [128, NF], F32)

    sems = {}
    for nm in ["qsy", "qw", "qxg", "qs", "qi", "qy", "SA", "SV", "SP"]:
        g = nc.semaphore(nm)
        h = g.__enter__()
        ctx_list.append(g)
        sems[nm] = h

    P = Prog()

    for (dst, src) in [(gcol, gcol_d), (bincol, bincol_d), (bdcol, bdcol_d),
                       (bocol, bocol_d), (cbcol, cbcol_d), (cwcol, cwcol_d),
                       (eyef, eyef_d), (eyeb, eyeb_d)]:
        P.op("sync", lambda e, dst=dst, src=src: e.dma_start(out=dst[:, :], in_=src[:, :]),
             inc="qsy", n=16)
    q_smalls = P.c("qsy")

    P.op("gpsimd", lambda e: e.dma_start(out=bcolb[:, :], in_=bcol_d[:, :]), inc="qw", n=16)
    q_bcolb = P.c("qw")
    for kk in range(KE):
        P.op("gpsimd",
             lambda e, kk=kk: e.dma_start(out=wd[:, kk, :],
                                          in_=wd_d[kk * 128:(kk + 1) * 128, :]),
             inc="qw", n=16)
    q_wd = P.c("qw")

    P.op("vector", lambda e: e.tensor_scalar_mul(nbdcol[:, :], bdcol[:, :], -1.0),
         waits=[("qsy", q_smalls)], inc="SV")
    P.op("vector", lambda e: e.tensor_copy(cin[:, :], bincol[:, :]), inc="SV")
    P.op("vector", lambda e: e.memset(vr[:, :, 0:2], 0.0), inc="SV")
    P.op("vector", lambda e: e.memset(fwd_c[:, :], 0.0), inc="SV")
    P.op("vector", lambda e: e.memset(bwd_c[:, :], 0.0), inc="SV")
    v_derived = P.c("SV")

    q_store_prev = 0
    v_cin_prev = 0
    for kk in range(KD):
        P.op("gpsimd",
             lambda e, kk=kk: e.dma_start(out=wstage[:, :],
                                          in_=win_d[kk * 128:(kk + 1) * 128, :]),
             waits=[("qsy", q_store_prev)], inc="qw", n=16)
        q_wst = P.c("qw")

        def cin_mms(e, kk=kk):
            last = None
            for f in range(NF):
                last = e.matmul(pc32[:, f:f + 1], wstage[:, f * 128:(f + 1) * 128],
                                bcolb[:, kk:kk + 1], start=True, stop=True)
            return last
        w_cin = [("qw", q_wst), ("qw", q_bcolb), ("SV", v_cin_prev)]
        P.op("tensor", cin_mms, waits=w_cin, inc="SP")
        p_cin = P.c("SP")
        P.op("vector", lambda e: e.tensor_add(cin[:, :], cin[:, :], pc32[:, :]),
             waits=[("SP", p_cin), ("SV", v_derived), ("SV", v_cin_prev)], inc="SV")
        v_cin_prev = P.c("SV")
        P.op("vector",
             lambda e, kk=kk: e.tensor_scalar_mul(wstage[:, :], wstage[:, :],
                                                  gcol[:, kk:kk + 1]),
             waits=[("SP", p_cin)], inc="SV")
        v_scaled = P.c("SV")
        P.op("sync",
             lambda e, kk=kk: e.dma_start(out=sc_win[:, :, kk, :],
                                          in_=wstage.rearrange("p (f c) -> p f c", f=NF)),
             waits=[("SV", v_scaled)], inc="qsy", n=16)
        q_store_prev = P.c("qsy")
    q_win_done = P.c("qsy")
    v_cin_done = v_cin_prev

    xr_last_SA = [0] * 4
    xr_last_SV = [0] * 4
    winr_last_mm = [0] * 4
    bank_evac = {0: 0, 1: 0}
    ptb_last_evac = 0
    a_spill, b_spill, f_spill, g_spill = [], [], [], []
    ar_scan = [0] * 8
    last_hcm_reader = 0
    vbf_readers = (0, 0)
    u_last_pe = [0, 0]
    sneg_last_v = [0, 0]
    cvt_last_silu = [0, 0]
    car_prev = {}
    gi_mm = 0

    c0, c1c, cm1, cE2, ctmp, cv1, cr1, cr2, crs2, cnm1, cA_, ccA = \
        0, 4, 8, 12, 16, 20, 24, 28, 32, 36, 40, 44

    for k in range(NCH):
        t0 = k * T
        xq_sub = []
        for s in range(NSUB):
            P.op("gpsimd",
                 lambda e, t0=t0, s=s: e.dma_start(
                     out=xr[:, s, :], in_=x_d[t0 + s * 128:t0 + (s + 1) * 128, :]),
                 waits=[("SA", xr_last_SA[s]), ("SV", xr_last_SV[s])], inc="qxg", n=16)
            xq_sub.append(P.c("qxg"))
        sq_m, red_m = [], []
        for s in range(NSUB):
            P.op("scalar",
                 lambda e, s=s: e.activation(out=xsq[:, :], in_=xr[:, s, :],
                                             func=AF.Square,
                                             accum_out=stA[:, c1c + s:c1c + s + 1]),
                 waits=[("qxg", xq_sub[s])], inc="SA")
            sq_m.append(P.c("SA"))
            P.op("vector",
                 lambda e, s=s: e.tensor_reduce(stA[:, c0 + s:c0 + s + 1], xr[:, s, :],
                                                AX.X, ALU.add),
                 waits=[("qxg", xq_sub[s])], inc="SV")
            red_m.append(P.c("SV"))
        def ch(fn, extra=()):
            P.op("vector", fn, waits=[("SV", P.c("SV"))] + list(extra), inc="SV")
            return P.c("SV")
        ch(lambda e: e.tensor_scalar_mul(stA[:, cm1:cm1 + 4], stA[:, c0:c0 + 4], 1.0 / D),
           extra=[("SA", sq_m[-1])])
        ch(lambda e: e.tensor_scalar_mul(stA[:, cE2:cE2 + 4], stA[:, c1c:c1c + 4], 1.0 / D))
        ch(lambda e: e.tensor_mul(stA[:, ctmp:ctmp + 4], stA[:, cm1:cm1 + 4], stA[:, cm1:cm1 + 4]))
        ch(lambda e: e.tensor_sub(stA[:, cv1:cv1 + 4], stA[:, cE2:cE2 + 4], stA[:, ctmp:ctmp + 4]))
        ch(lambda e: e.tensor_scalar_add(stA[:, ctmp:ctmp + 4], stA[:, cv1:cv1 + 4], EPS))
        ch(lambda e: e.reciprocal(stA[:, cr1:cr1 + 4], stA[:, ctmp:ctmp + 4]))
        ch(lambda e: e.tensor_mul(stA[:, ctmp:ctmp + 4], stA[:, cv1:cv1 + 4], stA[:, cr1:cr1 + 4]))
        ch(lambda e: e.tensor_scalar_add(stA[:, ctmp:ctmp + 4], stA[:, ctmp:ctmp + 4], EPS))
        v_recips = ch(lambda e: e.reciprocal(stA[:, cr2:cr2 + 4], stA[:, ctmp:ctmp + 4]))
        P.op("scalar",
             lambda e, k=k: e.activation(out=strs1[:, k * 4:(k + 1) * 4],
                                         in_=stA[:, cr1:cr1 + 4], func=AF.Sqrt),
             waits=[("SV", v_recips)])
        P.op("scalar", lambda e: e.activation(out=stA[:, crs2:crs2 + 4],
                                              in_=stA[:, cr2:cr2 + 4], func=AF.Sqrt),
             inc="SA")
        a_sqrts = P.c("SA")
        ch(lambda e: e.tensor_scalar_mul(stA[:, cnm1:cnm1 + 4], stA[:, cm1:cm1 + 4], -1.0),
           extra=[("SA", a_sqrts)])
        ch(lambda e, k=k: e.tensor_mul(stA[:, cA_:cA_ + 4], strs1[:, k * 4:(k + 1) * 4],
                                       stA[:, crs2:crs2 + 4]))
        ch(lambda e: e.tensor_mul(stA[:, ccA:ccA + 4], stA[:, cnm1:cnm1 + 4], stA[:, cA_:cA_ + 4]))
        v_cols = ch(lambda e, k=k: e.tensor_mul(stc1[:, k * 4:(k + 1) * 4],
                                                stA[:, cnm1:cnm1 + 4],
                                                strs1[:, k * 4:(k + 1) * 4]))

        for s in range(NSUB):
            P.op("scalar",
                 lambda e, s=s: e.activation(out=ur[:, s % 2, :], in_=xr[:, s, :],
                                             func=AF.Identity,
                                             scale=stA[:, cA_ + s:cA_ + s + 1],
                                             bias=stA[:, ccA + s:ccA + s + 1]),
                 waits=[("SV", v_cols), ("SP", u_last_pe[s % 2])], inc="SA")
            a_u = P.c("SA")
            xr_last_SA[s] = a_u
            xr_last_SV[s] = red_m[s]

            def htr(e, s=s):
                last = None
                for dd in range(KD):
                    last = e.transpose(ptb[:, dd * 128:(dd + 1) * 128],
                                       ur[:, s % 2, dd * 128:(dd + 1) * 128], eyeb[:, :])
                return last
            P.op("tensor", htr, waits=[("SA", a_u), ("SV", ptb_last_evac)], inc="SP")
            p_tr = P.c("SP")
            u_last_pe[s % 2] = p_tr
            P.op("vector",
                 lambda e, s=s: e.tensor_copy(h_cm[:, :, s * 128:(s + 1) * 128],
                                              ptb.rearrange("p (a b) -> p a b", a=KD)),
                 waits=[("SP", p_tr), ("SP", last_hcm_reader)], inc="SV")
            ptb_last_evac = P.c("SV")
        v_hcm = P.c("SV")

        for f in range(NF):
            mglob = k * NF + f
            slot = mglob % 4
            P.op("sync",
                 lambda e, f=f, slot=slot: e.dma_start(out=win_r[:, slot, :, :],
                                                       in_=sc_win[:, f, :, :]),
                 waits=[("SP", winr_last_mm[slot]), ("qsy", q_win_done)], inc="qsy", n=16)
            q_w = P.c("qsy")
            bank = gi_mm % 2
            pps = pA if bank == 0 else pB

            def mmgrp(e, slot=slot, pps=pps):
                last = None
                for kk in range(KD):
                    last = e.matmul(pps[:, :], win_r[:, slot, kk, :], h_cm[:, kk, :],
                                    start=(kk == 0), stop=(kk == KD - 1))
                return last
            P.op("tensor", mmgrp,
                 waits=[("qsy", q_w), ("SV", v_hcm), ("SA", bank_evac[bank])], inc="SP")
            p_grp = P.c("SP")
            gi_mm += 1
            winr_last_mm[slot] = p_grp
            if f == NF - 1:
                last_hcm_reader = p_grp

            if f < KE:
                ee = f
                w_ev = [("SP", p_grp)]
                if k > 0:
                    w_ev.append(("SV", car_prev[ee]))
                else:
                    w_ev.append(("SV", v_cin_done))
                P.op("scalar",
                     lambda e, ee=ee, pps=pps, f=f: e.activation(
                         out=vr[:, ee, 2:2 + T], in_=pps[:, :], func=AF.Identity,
                         bias=cin[:, f:f + 1]),
                     waits=w_ev, inc="SA")
                a_ev = P.c("SA")
                bank_evac[bank] = a_ev

                def conv(e, ee=ee):
                    e.tensor_scalar_mul(cvt[:, ee % 2, 0, :], vr[:, ee, 0:T],
                                        cwcol[:, 3 * ee:3 * ee + 1])
                    e.scalar_tensor_tensor(out=cvt[:, ee % 2, 1, :], in0=vr[:, ee, 1:1 + T],
                                           scalar=cwcol[:, 3 * ee + 1:3 * ee + 2],
                                           in1=cvt[:, ee % 2, 0, :],
                                           op0=ALU.mult, op1=ALU.add)
                    e.scalar_tensor_tensor(out=cvt[:, ee % 2, 0, :], in0=vr[:, ee, 2:2 + T],
                                           scalar=cwcol[:, 3 * ee + 2:3 * ee + 3],
                                           in1=cvt[:, ee % 2, 1, :],
                                           op0=ALU.mult, op1=ALU.add)
                    return e.tensor_copy(vr[:, ee, 0:2], vr[:, ee, T:T + 2])
                P.op("vector", conv,
                     waits=[("SA", a_ev), ("SA", cvt_last_silu[ee % 2])], inc="SV")
                car_prev[ee] = P.c("SV")
                w_si = [("SV", car_prev[ee])]
                if k > 0:
                    w_si += [("SP", vbf_readers[0]), ("SV", vbf_readers[1])]
                P.op("scalar",
                     lambda e, ee=ee: e.activation(out=v_bf[:, ee, :],
                                                   in_=cvt[:, ee % 2, 0, :],
                                                   func=AF.Silu, bias=cbcol[:, ee:ee + 1]),
                     waits=w_si, inc="SA")
                cvt_last_silu[ee % 2] = P.c("SA")
                if ee == KE - 1:
                    silu_all = P.c("SA")
            else:
                ee = f - KE
                grp = ee // 4
                s0 = (grp * 4) % 8
                slot8 = ee % 8
                g_glob = k * 4 + grp
                w_ev = [("SP", p_grp)]
                if g_glob >= 2:
                    w_ev.append(("qs", g_spill[g_glob - 2]))
                if k == 0:
                    w_ev.append(("SV", v_cin_done))
                P.op("scalar",
                     lambda e, pps=pps, f=f, slot8=slot8: e.activation(
                         out=sgr[:, slot8, :], in_=pps[:, :], func=AF.Sigmoid,
                         bias=cin[:, f:f + 1]),
                     waits=w_ev, inc="SA")
                bank_evac[bank] = P.c("SA")
                if ee % 4 == 3:
                    e0 = grp * 4
                    P.op("scalar",
                         lambda e, k=k, e0=e0, s0=s0: e.dma_start(
                             out=sc_g_v[:, e0:e0 + 4, k * T:(k + 1) * T],
                             in_=sgr[:, s0:s0 + 4, :]),
                         inc="qs", n=16)
                    g_spill.append(P.c("qs"))

        for ee in range(KE):
            bank = gi_mm % 2
            pps = pA if bank == 0 else pB
            grp = ee // 4
            s0 = (grp * 4) % 8
            slot8 = ee % 8
            g_glob = k * 4 + grp

            def dmm(e, ee=ee, pps=pps):
                last = None
                for kk in range(KE):
                    last = e.matmul(pps[:, :], wd[:, kk, ee * 128:(ee + 1) * 128],
                                    v_bf[:, kk, :], start=(kk == 0), stop=(kk == KE - 1))
                return last
            w_mm = [("SA", silu_all), ("SA", bank_evac[bank])]
            if k == 0 and ee == 0:
                w_mm.append(("qw", q_wd))
            P.op("tensor", dmm, waits=w_mm, inc="SP")
            p_grp = P.c("SP")
            gi_mm += 1

            w_a = [("SP", p_grp), ("SV", ar_scan[slot8])]
            if g_glob >= 2:
                w_a.append(("qs", a_spill[g_glob - 2]))
            P.op("scalar",
                 lambda e, ee=ee, pps=pps, slot8=slot8: e.activation(
                     out=ar[:, slot8, :], in_=pps[:, :], func=AF.Sigmoid,
                     bias=bdcol[:, ee:ee + 1]),
                 waits=w_a, inc="SA")
            P.op("scalar",
                 lambda e, ee=ee, pps=pps: e.activation(
                     out=sneg[:, ee % 2, :], in_=pps[:, :], func=AF.Sigmoid,
                     scale=-1.0, bias=nbdcol[:, ee:ee + 1]),
                 waits=[("SV", sneg_last_v[ee % 2])], inc="SA")
            a_pair = P.c("SA")
            bank_evac[bank] = a_pair

            w_b = [("SA", a_pair)]
            if g_glob >= 2:
                w_b.append(("qs", b_spill[g_glob - 2]))
            P.op("vector",
                 lambda e, ee=ee, slot8=slot8: e.tensor_mul(
                     br[:, slot8, :], sneg[:, ee % 2, :], v_bf[:, ee, :]),
                 waits=w_b, inc="SV")
            sneg_last_v[ee % 2] = P.c("SV")

            w_s = []
            if g_glob >= 2:
                w_s.append(("qs", f_spill[g_glob - 2]))

            P.op("vector",
                 lambda e, ee=ee, slot8=slot8: e.tensor_tensor_scan(
                     out=fr[:, slot8, :], data0=ar[:, slot8, :], data1=br[:, slot8, :],
                     initial=fwd_c[:, ee:ee + 1], op0=ALU.mult, op1=ALU.add),
                 waits=w_s, inc="SV")
            v_scan0 = P.c("SV")
            P.op("vector",
                 lambda e, ee=ee, slot8=slot8: e.tensor_copy(fwd_c[:, ee:ee + 1],
                                                             fr[:, slot8, T - 1:T]),
                 waits=[("SV", v_scan0)], inc="SV")
            v_scan = P.c("SV")
            ar_scan[slot8] = v_scan

            if ee % 4 == 3:
                e0 = grp * 4
                P.op("scalar",
                     lambda e, k=k, e0=e0, s0=s0: e.dma_start(
                         out=sc_a_v[:, e0:e0 + 4, k * T:(k + 1) * T], in_=ar[:, s0:s0 + 4, :]),
                     waits=[("SV", v_scan)], inc="qs", n=16)
                a_spill.append(P.c("qs"))
                P.op("scalar",
                     lambda e, k=k, e0=e0, s0=s0: e.dma_start(
                         out=sc_b_v[:, e0:e0 + 4, k * T:(k + 1) * T], in_=br[:, s0:s0 + 4, :]),
                     inc="qs", n=16)
                b_spill.append(P.c("qs"))
                P.op("scalar",
                     lambda e, k=k, e0=e0, s0=s0: e.dma_start(
                         out=sc_f_v[:, e0:e0 + 4, k * T:(k + 1) * T], in_=fr[:, s0:s0 + 4, :]),
                     inc="qs", n=16)
                f_spill.append(P.c("qs"))
        vbf_readers = (P.c("SP"), P.c("SV"))

    a_done = [("SP", P.c("SP")), ("SA", P.c("SA")), ("SV", P.c("SV")), ("qs", P.c("qs"))]

    for ee in range(KE):
        P.op("gpsimd",
             lambda e, ee=ee: e.dma_start(out=wout[:, ee, :],
                                          in_=wo_d[ee * 128:(ee + 1) * 128, :]),
             waits=a_done if ee == 0 else [], inc="qw", n=16)
    for kk in range(KD):
        P.op("gpsimd",
             lambda e, kk=kk: e.dma_start(out=weff[:, kk, :],
                                          in_=wft_d[kk * 128:(kk + 1) * 128, :]),
             inc="qw", n=16)
    P.op("gpsimd", lambda e: e.dma_start(out=dishab[:, :], in_=disha_d[:, :]), inc="qw", n=16)
    q_bw = P.c("qw")

    def weff_add(e):
        last = None
        for kk in range(KD):
            last = e.tensor_add(weff[:, kk, :], weff[:, kk, :], dishab[:, :])
        return last
    P.op("vector", weff_add, waits=a_done + [("qw", q_bw)], inc="SV")
    v_weff = P.c("SV")

    if serial_b:
        P.serial = True
    if debug:
        dbg_fsall = nc.dram_tensor("dbg_fsall", [NCH * 128, KE * T], BF16,
                                   kind="ExternalOutput").ap()
        dbg_yall = nc.dram_tensor("dbg_yall", [NCH * 128, NSUB * D], F32,
                                  kind="ExternalOutput").ap()
    if debug:
        dbg_bwd = nc.dram_tensor("dbg_bwd", [NCH * 128, KE * T], BF16,
                                 kind="ExternalOutput").ap()
    dump_fs_q = 0
    dump_y_q = 0
    dump_bb_q = [0, 0]
    in_ring_ok = {"a": [0] * 8, "b": [0] * 8, "f": [0] * 8, "g": [0] * 8}
    xB_gate = (0, 0)
    yB_last_SA = 0
    x1_last_SA = 0
    aftm_attn_gate = 0
    ptr_last_evac_SA = 0
    aor_last_SP = [0, 0]
    lny_last_SV = [0, 0]
    l1r_last_SP = [0, 0]
    ln1cm_last_SP = 0
    fs_last_SP = 0
    outq = [0, 0, 0, 0]

    cb0, cb1, cbm, cbE, cbt, cbv, cbr = 0, 4, 8, 12, 16, 20, 24

    for ki in range(NCH):
        k = NCH - 1 - ki
        t0 = k * T
        q_in = {}

        def emit_group_load(grp, ki=ki, k=k):
            e0 = grp * 4
            s0 = e0 % 8
            for nm, sc_v, dst in [("a", sc_a_v, ain), ("b", sc_b_v, binb),
                                  ("f", sc_f_v, fwdin), ("g", sc_g_v, sgin)]:
                w = [("SV", max(in_ring_ok[nm][s0 + j] for j in range(4)))]
                if serial_loads:
                    w = w + [(s, v) for s, v in P.cnt.items() if v > 0]
                if ki == 0 and grp == 0:
                    w = a_done + w
                P.op("sync",
                     lambda e, sc_v=sc_v, dst=dst, e0=e0, s0=s0, k=k: e.dma_start(
                         out=dst[:, s0:s0 + 4, :],
                         in_=sc_v[:, e0:e0 + 4, k * T:(k + 1) * T]),
                     waits=w, inc="qi", n=16)
                q_in[(nm, grp)] = P.c("qi")
        emit_group_load(0)
        emit_group_load(1)
        wxl = [("SV", xB_gate[0]), ("qy", xB_gate[1])]
        if ki == 0:
            wxl = a_done + wxl
        P.op("sync",
             lambda e, t0=t0: e.dma_start(
                 out=xB[:, :, :],
                 in_=x_d[t0:t0 + T, :].rearrange("(a p) b -> p a b", p=128)),
             waits=wxl, inc="qi", n=16)
        q_x = P.c("qi")

        for ee in range(KE):
            grp = ee // 4
            s0 = ee % 8
            sl4 = ee % 4

            w_bs = [("qi", q_in[("a", grp)]), ("qi", q_in[("b", grp)])]
            if ki > 0 and ee == 0:
                w_bs.append(("SP", fs_last_SP))
                w_bs.append(("qsy", dump_fs_q))
            if debug:
                w_bs.append(("qsy", dump_bb_q[0]))
            P.op("vector",
                 lambda e, ee=ee, s0=s0, sl4=sl4: e.tensor_tensor_scan(
                     out=bb[:, sl4, :], data0=ain[:, s0, ::-1], data1=binb[:, s0, ::-1],
                     initial=bwd_c[:, ee:ee + 1], op0=ALU.mult, op1=ALU.add),
                 waits=w_bs, inc="SV")
            v_bs0 = P.c("SV")
            P.op("vector",
                 lambda e, ee=ee, sl4=sl4: e.tensor_copy(bwd_c[:, ee:ee + 1],
                                                         bb[:, sl4, T - 1:T]),
                 waits=[("SV", v_bs0)], inc="SV")
            v_b = P.c("SV")
            in_ring_ok["a"][s0] = v_b
            in_ring_ok["b"][s0] = v_b

            def fuse(e, ee=ee, s0=s0, sl4=sl4):
                e.tensor_add(tbb[:, :], fwdin[:, s0, :], bb[:, sl4, ::-1])
                return e.tensor_mul(fs_bf[:, ee, :], tbb[:, :], sgin[:, s0, :])
            P.op("vector", fuse,
                 waits=[("qi", q_in[("f", grp)]), ("qi", q_in[("g", grp)]),
                        ("SV", v_b)], inc="SV")
            v_f = P.c("SV")
            in_ring_ok["f"][s0] = v_f
            in_ring_ok["g"][s0] = v_f
            if debug and ee % 4 == 3:
                grp4 = ee // 4
                P.op("sync",
                     lambda e, k=k, grp4=grp4: e.dma_start(
                         out=dbg_bwd[k * 128:(k + 1) * 128,
                                     grp4 * 4 * T:(grp4 + 1) * 4 * T],
                         in_=bb.rearrange("p a b -> p (a b)")),
                     waits=[("SV", P.c("SV"))], inc="qsy", n=16)
                dump_bb_q[0] = P.c("qsy")
            if ee == 3:
                emit_group_load(2)
            elif ee == 7:
                emit_group_load(3)
        v_fused_all = P.c("SV")
        if debug:
            P.op("sync",
                 lambda e, k=k: e.dma_start(
                     out=dbg_fsall[k * 128:(k + 1) * 128, :],
                     in_=fs_bf.rearrange("p a b -> p (a b)")),
                 waits=[("SV", v_fused_all)], inc="qsy", n=16)
            dump_fs_q = P.c("qsy")

        for dd in range(KD):
            bank = gi_mm % 2
            pps = pA if bank == 0 else pB

            def omm(e, dd=dd, pps=pps):
                last = None
                for kk in range(KE):
                    last = e.matmul(pps[:, :], wout[:, kk, dd * 128:(dd + 1) * 128],
                                    fs_bf[:, kk, :], start=(kk == 0), stop=(kk == KE - 1))
                return last
            w_mm = [("SV", v_fused_all), ("SA", bank_evac[bank])]
            if ki == 0 and dd == 0:
                w_mm.append(("qw", q_bw))
            P.op("tensor", omm, waits=w_mm, inc="SP")
            p_grp = P.c("SP")
            gi_mm += 1
            P.op("scalar",
                 lambda e, dd=dd, pps=pps: e.activation(out=aor[:, dd % 2, :], in_=pps[:, :],
                                                        func=AF.Identity,
                                                        bias=bocol[:, dd:dd + 1]),
                 waits=[("SP", p_grp), ("SP", aor_last_SP[dd % 2])], inc="SA")
            a_ev = P.c("SA")
            bank_evac[bank] = a_ev

            def otr(e, dd=dd):
                last = None
                for s in range(NSUB):
                    last = e.transpose(ptr[:, s * 128:(s + 1) * 128],
                                       aor[:, dd % 2, s * 128:(s + 1) * 128], eyef[:, :])
                return last
            P.op("tensor", otr, waits=[("SA", a_ev), ("SA", ptr_last_evac_SA)], inc="SP")
            p_tr = P.c("SP")
            aor_last_SP[dd % 2] = p_tr

            def oev(e, dd=dd, k=k):
                last = None
                for s in range(NSUB):
                    last = e.activation(out=aftm[:, s, dd * 128:(dd + 1) * 128],
                                        in_=ptr[:, s * 128:(s + 1) * 128],
                                        func=AF.Identity,
                                        bias=stc1[:, k * 4 + s:k * 4 + s + 1])
                return last
            P.op("scalar", oev, waits=[("SP", p_tr)], inc="SA")
            ptr_last_evac_SA = P.c("SA")
        ev_attn = P.c("SA")

        yred, ysq = [], []
        for s in range(NSUB):
            P.op("vector",
                 lambda e, s=s, k=k: e.scalar_tensor_tensor(
                     out=yB[:, s, :], in0=xB[:, s, :],
                     scalar=strs1[:, k * 4 + s:k * 4 + s + 1], in1=aftm[:, s, :],
                     op0=ALU.mult, op1=ALU.add),
                 waits=[("SA", ev_attn), ("qi", q_x), ("SA", yB_last_SA),
                        ("qsy", dump_y_q)], inc="SV")
            v_y = P.c("SV")
            P.op("scalar",
                 lambda e, s=s: e.activation(out=xsq[:, :], in_=yB[:, s, :], func=AF.Square,
                                             accum_out=stB[:, cb1 + s:cb1 + s + 1]),
                 waits=[("SV", v_y)], inc="SA")
            ysq.append(P.c("SA"))
            P.op("vector",
                 lambda e, s=s: e.tensor_reduce(stB[:, cb0 + s:cb0 + s + 1], yB[:, s, :],
                                                AX.X, ALU.add), inc="SV")
            yred.append(P.c("SV"))

        def colmathB(rs_lo, cc_lo, red_last, sq_last):
            def chb(fn, extra=()):
                P.op("vector", fn, waits=[("SV", P.c("SV"))] + list(extra), inc="SV")
                return P.c("SV")
            chb(lambda e: e.tensor_scalar_mul(stB[:, cbm:cbm + 4], stB[:, cb0:cb0 + 4], 1.0 / D),
                extra=[("SV", red_last), ("SA", sq_last)])
            chb(lambda e: e.tensor_scalar_mul(stB[:, cbE:cbE + 4], stB[:, cb1:cb1 + 4], 1.0 / D))
            chb(lambda e: e.tensor_mul(stB[:, cbt:cbt + 4], stB[:, cbm:cbm + 4], stB[:, cbm:cbm + 4]))
            chb(lambda e: e.tensor_sub(stB[:, cbv:cbv + 4], stB[:, cbE:cbE + 4], stB[:, cbt:cbt + 4]))
            chb(lambda e: e.tensor_scalar_add(stB[:, cbt:cbt + 4], stB[:, cbv:cbv + 4], EPS))
            v_r = chb(lambda e: e.reciprocal(stB[:, cbr:cbr + 4], stB[:, cbt:cbt + 4]))
            P.op("scalar",
                 lambda e, rs_lo=rs_lo: e.activation(out=stB[:, rs_lo:rs_lo + 4],
                                                     in_=stB[:, cbr:cbr + 4], func=AF.Sqrt),
                 waits=[("SV", v_r)], inc="SA")
            a_rs = P.c("SA")
            chb(lambda e: e.tensor_scalar_mul(stB[:, cbt:cbt + 4], stB[:, cbm:cbm + 4], -1.0),
                extra=[("SA", a_rs)])
            return chb(lambda e, rs_lo=rs_lo, cc_lo=cc_lo: e.tensor_mul(
                stB[:, cc_lo:cc_lo + 4], stB[:, cbt:cbt + 4], stB[:, rs_lo:rs_lo + 4]))

        if debug:
            P.op("sync",
                 lambda e, k=k: e.dma_start(
                     out=dbg_yall[k * 128:(k + 1) * 128, :],
                     in_=yB.rearrange("p a b -> p (a b)")),
                 waits=[("SV", P.c("SV"))], inc="qsy", n=16)
            dump_y_q = P.c("qsy")
        v_ycols = colmathB(28, 36, yred[-1], ysq[-1])

        x1red, x1sq = [], []
        for s in range(NSUB):
            P.op("scalar",
                 lambda e, s=s: e.activation(out=lnyr[:, s % 2, :], in_=yB[:, s, :],
                                             func=AF.Identity,
                                             scale=stB[:, 28 + s:29 + s],
                                             bias=stB[:, 36 + s:37 + s]),
                 waits=[("SV", v_ycols), ("SV", lny_last_SV[s % 2])], inc="SA")
            a_l = P.c("SA")
            yB_last_SA = a_l
            P.op("vector",
                 lambda e, s=s: e.tensor_add(x1[:, s, :], lnyr[:, s % 2, :], xB[:, s, :]),
                 waits=[("SA", a_l), ("SA", x1_last_SA)], inc="SV")
            v_x1 = P.c("SV")
            lny_last_SV[s % 2] = v_x1
            P.op("scalar",
                 lambda e, s=s: e.activation(out=xsq[:, :], in_=x1[:, s, :], func=AF.Square,
                                             accum_out=stB[:, cb1 + s:cb1 + s + 1]),
                 waits=[("SV", v_x1)], inc="SA")
            x1sq.append(P.c("SA"))
            P.op("vector",
                 lambda e, s=s: e.tensor_reduce(stB[:, cb0 + s:cb0 + s + 1], x1[:, s, :],
                                                AX.X, ALU.add), inc="SV")
            x1red.append(P.c("SV"))
        v_x1cols = colmathB(40, 44, x1red[-1], x1sq[-1])

        for s in range(NSUB):
            P.op("scalar",
                 lambda e, s=s: e.activation(out=l1r[:, s % 2, :], in_=x1[:, s, :],
                                             func=AF.Identity,
                                             scale=stB[:, 40 + s:41 + s],
                                             bias=stB[:, 44 + s:45 + s]),
                 waits=[("SV", v_x1cols), ("SP", l1r_last_SP[s % 2])], inc="SA")
            a_l1 = P.c("SA")

            def l1tr(e, s=s):
                last = None
                for dd in range(KD):
                    last = e.transpose(ptb[:, dd * 128:(dd + 1) * 128],
                                       l1r[:, s % 2, dd * 128:(dd + 1) * 128], eyeb[:, :])
                return last
            P.op("tensor", l1tr, waits=[("SA", a_l1), ("SV", ptb_last_evac)], inc="SP")
            p_t = P.c("SP")
            l1r_last_SP[s % 2] = p_t
            P.op("vector",
                 lambda e, s=s: e.tensor_copy(ln1cm[:, :, s * 128:(s + 1) * 128],
                                              ptb.rearrange("p (a b) -> p a b", a=KD)),
                 waits=[("SP", p_t), ("SP", ln1cm_last_SP)], inc="SV")
            ptb_last_evac = P.c("SV")
        v_ln1 = P.c("SV")

        for dd in range(KD):
            bank = gi_mm % 2
            pps = pA if bank == 0 else pB

            def fmm(e, dd=dd, pps=pps):
                last = None
                for kk in range(KD):
                    last = e.matmul(pps[:, :], weff[:, kk, dd * 128:(dd + 1) * 128],
                                    ln1cm[:, kk, :], start=(kk == 0), stop=(kk == KD - 1))
                return last
            w_mm = [("SV", v_ln1), ("SA", bank_evac[bank])]
            if ki == 0 and dd == 0:
                w_mm.append(("SV", v_weff))
            P.op("tensor", fmm, waits=w_mm, inc="SP")
            p_grp = P.c("SP")
            gi_mm += 1
            if dd == KD - 1:
                ln1cm_last_SP = p_grp
            P.op("scalar",
                 lambda e, dd=dd, pps=pps: e.copy(aor[:, dd % 2, :], pps[:, :]),
                 waits=[("SP", p_grp), ("SP", aor_last_SP[dd % 2])], inc="SA")
            a_ev = P.c("SA")
            bank_evac[bank] = a_ev

            def ftr(e, dd=dd):
                last = None
                for s in range(NSUB):
                    last = e.transpose(ptr[:, s * 128:(s + 1) * 128],
                                       aor[:, dd % 2, s * 128:(s + 1) * 128], eyef[:, :])
                return last
            P.op("tensor", ftr, waits=[("SA", a_ev), ("SA", ptr_last_evac_SA)], inc="SP")
            p_tr = P.c("SP")
            aor_last_SP[dd % 2] = p_tr

            def fev(e, dd=dd):
                last = None
                for s in range(NSUB):
                    last = e.copy(aftm[:, s, dd * 128:(dd + 1) * 128],
                                  ptr[:, s * 128:(s + 1) * 128])
                return last
            P.op("scalar", fev, waits=[("SP", p_tr)], inc="SA")
            ptr_last_evac_SA = P.c("SA")
        ev_f = P.c("SA")

        fred, fsq = [], []
        for s in range(NSUB):
            P.op("scalar",
                 lambda e, s=s: e.activation(out=xsq[:, :], in_=aftm[:, s, :], func=AF.Square,
                                             accum_out=stB[:, cb1 + s:cb1 + s + 1]),
                 waits=[("SA", ev_f)], inc="SA")
            fsq.append(P.c("SA"))
            P.op("vector",
                 lambda e, s=s: e.tensor_reduce(stB[:, cb0 + s:cb0 + s + 1], aftm[:, s, :],
                                                AX.X, ALU.add),
                 waits=[("SA", ev_f)], inc="SV")
            fred.append(P.c("SV"))
        v_fcols = colmathB(28, 36, fred[-1], fsq[-1])

        for s in range(NSUB):
            P.op("scalar",
                 lambda e, s=s: e.activation(out=lnyr[:, s % 2, :], in_=aftm[:, s, :],
                                             func=AF.Identity,
                                             scale=stB[:, 28 + s:29 + s],
                                             bias=stB[:, 36 + s:37 + s]),
                 waits=[("SV", v_fcols), ("SV", lny_last_SV[s % 2])], inc="SA")
            a_lf = P.c("SA")
            P.op("vector",
                 lambda e, s=s: e.tensor_add(xB[:, s, :], lnyr[:, s % 2, :], x1[:, s, :]),
                 waits=[("SA", a_lf), ("qy", outq[s])], inc="SV")
            v_o = P.c("SV")
            lny_last_SV[s % 2] = v_o
            P.op("scalar",
                 lambda e, s=s, t0=t0: e.dma_start(
                     out=y_d[t0 + s * 128:t0 + (s + 1) * 128, :], in_=xB[:, s, :]),
                 waits=[("SV", v_o)], inc="qy", n=16)
            outq[s] = P.c("qy")
        xB_gate = (P.c("SV"), outq[3])
        x1_last_SA = P.c("SA")
        fs_last_SP = P.c("SP")

    if debug:
        for nm_t, ap_t in [("dbg_yB", yB), ("dbg_x1", x1), ("dbg_aftm", aftm),
                           ("dbg_fs", fs_bf), ("dbg_ln1", ln1cm)]:
            d_t = nc.dram_tensor(nm_t, [128, ap_t.free_size()],
                                 ap_t.dtype, kind="ExternalOutput").ap()
            P.op("sync",
                 lambda e, d_t=d_t, ap_t=ap_t: e.dma_start(
                     out=d_t[:, :], in_=ap_t.rearrange(
                         "p a b -> p (a b)" if ap_t.ndim == 3 else "p a -> p a")),
                 waits=[("SA", P.c("SA")), ("SV", P.c("SV")), ("SP", P.c("SP"))],
                 inc="qsy", n=16)
        dbg_cin = nc.dram_tensor("dbg_cin", [128, NF], F32, kind="ExternalOutput").ap()
        dbg_rs1 = nc.dram_tensor("dbg_rs1", [128, NSUB * NCH], F32, kind="ExternalOutput").ap()
        dbg_c1 = nc.dram_tensor("dbg_c1", [128, NSUB * NCH], F32, kind="ExternalOutput").ap()
        wlast = [("SA", P.c("SA")), ("SV", P.c("SV"))]
        P.op("sync", lambda e: e.dma_start(out=dbg_cin[:, :], in_=cin[:, :]),
             waits=wlast, inc="qsy", n=16)
        P.op("sync", lambda e: e.dma_start(out=dbg_rs1[:, :], in_=strs1[:, :]), inc="qsy", n=16)
        P.op("sync", lambda e: e.dma_start(out=dbg_c1[:, :], in_=stc1[:, :]), inc="qsy", n=16)
    P.op("sync", lambda e: e.nop(),
         waits=[("qy", P.c("qy")), ("qs", P.c("qs")), ("qi", P.c("qi")), ("qsy", P.c("qsy"))])

    with nc.Block() as block:
        for eng_name in ("sync", "scalar", "vector", "tensor", "gpsimd"):
            def body(eng, eng_name=eng_name):
                done = {}
                for (e_nm, waits, fn, inc, n) in P.ops:
                    if e_nm != eng_name:
                        continue
                    for (snm, v) in waits:
                        if done.get(snm, 0) >= v:
                            continue
                        eng.wait_ge(sems[snm], v)
                        done[snm] = v
                    ins = fn(eng)
                    if inc is not None:
                        ins.then_inc(sems[inc], n)
            getattr(block, eng_name)(body)

    for g in reversed(ctx_list):
        g.__exit__(None, None, None)
    return nc


def make_in_maps(inputs, S):
    x = np.ascontiguousarray(np.asarray(inputs["L0_attn_x"], dtype=np.float32))
    g = np.asarray(inputs["L0_attn_ln_gamma"], dtype=np.float32)
    b = np.asarray(inputs["L0_attn_ln_beta"], dtype=np.float32)
    Win = np.ascontiguousarray(np.asarray(inputs["L0_attn_W_in"], dtype=np.float32))
    b_in = np.asarray(inputs["L0_attn_b_in"], dtype=np.float32)
    conv_w = np.asarray(inputs["L0_attn_conv_w"], dtype=np.float32)
    conv_b = np.asarray(inputs["L0_attn_conv_b"], dtype=np.float32)
    Wd = np.ascontiguousarray(np.asarray(inputs["L0_attn_W_delta"], dtype=np.float32))
    b_d = np.asarray(inputs["L0_attn_b_delta"], dtype=np.float32)
    Wo = np.ascontiguousarray(np.asarray(inputs["L0_attn_W_out"], dtype=np.float32))
    b_o = np.asarray(inputs["L0_attn_b_out"], dtype=np.float32)
    Wf = np.asarray(inputs["L0_ffn_weight"], dtype=np.float32)
    disha = np.ascontiguousarray(np.asarray(inputs["L0_ffn_disha"], dtype=np.float32))

    common = {
        "Win": Win,
        "Wd": Wd,
        "Wo": Wo,
        "WfT": np.ascontiguousarray(Wf.T),
        "disha": np.ascontiguousarray(np.concatenate([disha, disha], axis=0)),
        "gcol": np.ascontiguousarray(g.reshape(KD, 128).T),
        "bcol": np.ascontiguousarray(b.reshape(KD, 128).T),
        "bincol": np.ascontiguousarray(b_in.reshape(NF, 128).T),
        "bdcol": np.ascontiguousarray(b_d.reshape(KE, 128).T),
        "bocol": np.ascontiguousarray(b_o.reshape(KD, 128).T),
        "cbcol": np.ascontiguousarray(conv_b.reshape(KE, 128).T),
        "cwcol": np.ascontiguousarray(
            conv_w.reshape(KE, 128, 3).transpose(1, 0, 2).reshape(128, 3 * KE)),
        "eyef": np.eye(128, dtype=np.float32),
        "eyeb": np.eye(128, dtype=bfdt),
    }
    return [dict(common, x=np.ascontiguousarray(x[c, :S, :])) for c in range(8)]


S_FULL = 4096
_PROG_CACHE = {}


def kernel(**inputs):
    from concourse.bass_utils import run_bass_kernel_spmd

    if S_FULL not in _PROG_CACHE:
        _PROG_CACHE[S_FULL] = build_program(S_FULL)
    nc = _PROG_CACHE[S_FULL]
    in_maps = make_in_maps(inputs, S_FULL)
    res = run_bass_kernel_spmd(nc, in_maps, core_ids=list(range(8)))
    return np.stack([res.results[c]["y"] for c in range(8)], axis=0)


# version 03
# speedup vs baseline: 1.0161x; 1.0161x over previous
import numpy as np
import ml_dtypes
import concourse.bass as bass
import concourse.mybir as mybir

F32 = mybir.dt.float32
BF16 = mybir.dt.bfloat16
AF = mybir.ActivationFunctionType
ALU = mybir.AluOpType
AX = mybir.AxisListType

D = 1024
INNER = 2048
FF = 4096
KD = 8
KE = 16
NF = 32
T = 512
NSUB = 4
EPS = 1e-5
bfdt = ml_dtypes.bfloat16


class Prog:
    def __init__(self):
        self.ops = []
        self.cnt = {}
        self.serial = False

    def c(self, sem):
        return self.cnt.get(sem, 0)

    def op(self, eng, fn, waits=(), inc=None, n=1):
        ws = [(s, v) for (s, v) in waits if v and v > 0]
        if self.serial:
            ws = ws + [(s, v) for s, v in self.cnt.items() if v > 0]
        self.ops.append((eng, ws, fn, inc, n))
        if inc is not None:
            self.cnt[inc] = self.c(inc) + n
        return self.c(inc) if inc is not None else None


def carve(arena_ap, off, shape, dtype):
    n = 1
    for s in shape:
        n *= s
    if dtype == F32:
        sl = arena_ap[:, off:off + 2 * n].bitcast(F32)
    else:
        assert dtype == BF16
        sl = arena_ap[:, off:off + n]
    if len(shape) == 1:
        return sl
    pat = " ".join(f"a{i}" for i in range(len(shape)))
    kw = {f"a{i}": shape[i] for i in range(len(shape))}
    return sl.rearrange(f"p ({pat}) -> p {pat}", **kw)


def build_program(S, debug=False, serial_b=False, serial_loads=False):
    NCH = S // T
    nc = bass.Bass("TRN2", target_bir_lowering=False)
    sckind = "ExternalOutput" if debug else "Internal"

    x_d = nc.dram_tensor("x", [S, D], F32, kind="ExternalInput").ap()
    win_d = nc.dram_tensor("Win", [D, FF], F32, kind="ExternalInput").ap()
    wd_d = nc.dram_tensor("Wd", [INNER, INNER], F32, kind="ExternalInput").ap()
    wo_d = nc.dram_tensor("Wo", [INNER, D], F32, kind="ExternalInput").ap()
    wft_d = nc.dram_tensor("WfT", [D, D], F32, kind="ExternalInput").ap()
    disha_d = nc.dram_tensor("disha", [128, D], F32, kind="ExternalInput").ap()
    gcol_d = nc.dram_tensor("gcol", [128, KD], F32, kind="ExternalInput").ap()
    bcol_d = nc.dram_tensor("bcol", [128, KD], F32, kind="ExternalInput").ap()
    bincol_d = nc.dram_tensor("bincol", [128, NF], F32, kind="ExternalInput").ap()
    bdcol_d = nc.dram_tensor("bdcol", [128, KE], F32, kind="ExternalInput").ap()
    bocol_d = nc.dram_tensor("bocol", [128, KD], F32, kind="ExternalInput").ap()
    cbcol_d = nc.dram_tensor("cbcol", [128, KE], F32, kind="ExternalInput").ap()
    cwcol_d = nc.dram_tensor("cwcol", [128, 3 * KE], F32, kind="ExternalInput").ap()
    eyef_d = nc.dram_tensor("eyef", [128, 128], F32, kind="ExternalInput").ap()
    eyeb_d = nc.dram_tensor("eyeb", [128, 128], BF16, kind="ExternalInput").ap()
    y_d = nc.dram_tensor("y", [S, D], F32, kind="ExternalOutput").ap()

    sc_a = nc.dram_tensor("sc_a", [INNER, S], BF16, kind=sckind).ap()
    sc_b = nc.dram_tensor("sc_b", [INNER, S], BF16, kind=sckind).ap()
    sc_f = nc.dram_tensor("sc_f", [INNER, S], BF16, kind=sckind).ap()
    sc_g = nc.dram_tensor("sc_g", [INNER, S], BF16, kind=sckind).ap()
    sc_win = nc.dram_tensor("sc_win", [128, NF, KD, 128], BF16, kind="Internal").ap()

    sc_a_v = sc_a.rearrange("(e p) s -> p e s", p=128)
    sc_b_v = sc_b.rearrange("(e p) s -> p e s", p=128)
    sc_f_v = sc_f.rearrange("(e p) s -> p e s", p=128)
    sc_g_v = sc_g.rearrange("(e p) s -> p e s", p=128)

    ctx_list = []

    def sb(name, shape, dtype):
        g = nc.sbuf_tensor("s_" + name, shape, dtype)
        h = g.__enter__()
        ctx_list.append(g)
        return h.ap()

    gcol = sb("gcol", [128, KD], F32)
    bcolb = sb("bcolb", [128, KD], BF16)
    bincol = sb("bincol", [128, NF], F32)
    bdcol = sb("bdcol", [128, KE], F32)
    nbdcol = sb("nbdcol", [128, KE], F32)
    bocol = sb("bocol", [128, KD], F32)
    cbcol = sb("cbcol", [128, KE], F32)
    cwcol = sb("cwcol", [128, 3 * KE], F32)
    cin = sb("cin", [128, NF], F32)
    eyef = sb("eyef", [128, 128], F32)
    eyeb = sb("eyeb", [128, 128], BF16)
    dishab = sb("dishab", [128, D], BF16)
    strs1 = sb("strs1", [128, NSUB * NCH], F32)
    stc1 = sb("stc1", [128, NSUB * NCH], F32)
    stA = sb("stA", [128, 64], F32)
    stB = sb("stB", [128, 64], F32)
    fwd_c = sb("fwd_c", [128, KE], F32)
    bwd_c = sb("bwd_c", [128, KE], F32)
    xsq = sb("xsq", [128, D], F32)

    AR = 98304
    arena = sb("arena", [128, AR], BF16)

    wd = carve(arena, 0, [KE, INNER], BF16)
    win_r = carve(arena, 32768, [4, KD, 128], BF16)
    xr = carve(arena, 36864, [4, D], F32)
    h_cm = carve(arena, 45056, [KD, T], BF16)
    vr = carve(arena, 49152, [KE, T + 4], BF16)
    v_bf = carve(arena, 57408, [KE, T], BF16)
    ar = carve(arena, 65600, [8, T], BF16)
    br = carve(arena, 69696, [8, T], BF16)
    fr = carve(arena, 73792, [8, T], BF16)
    sgr = carve(arena, 77888, [8, T], BF16)
    cvt = carve(arena, 81984, [2, 2, T], F32)
    ur = carve(arena, 92224, [4, D], BF16)
    sneg = carve(arena, 90176, [2, T], F32)
    wstage = carve(arena, 65600, [FF], BF16)

    wout = carve(arena, 0, [KE, D], BF16)
    weff = carve(arena, 16384, [KD, D], BF16)
    ain = carve(arena, 24576, [8, T], BF16)
    binb = carve(arena, 28672, [8, T], BF16)
    fwdin = carve(arena, 32768, [8, T], BF16)
    sgin = carve(arena, 36864, [8, T], BF16)
    fs_bf = carve(arena, 40960, [KE, T], BF16)
    xB = carve(arena, 49152, [NSUB, D], F32)
    yB = carve(arena, 57344, [NSUB, D], F32)
    x1 = carve(arena, 65536, [NSUB, D], F32)
    aftm = carve(arena, 73728, [NSUB, D], F32)
    ln1cm = carve(arena, 81920, [KD, T], BF16)
    bb = carve(arena, 86016, [4, T], BF16)
    lnyr = carve(arena, 88064, [2, D], F32)
    l1r = carve(arena, 92160, [2, D], BF16)
    aor = carve(arena, 94208, [2, T], F32)
    tbb = carve(arena, 96256, [T], BF16)

    def psb(name, shape, dtype):
        g = nc.psum_tensor(name, shape, dtype)
        h = g.__enter__()
        ctx_list.append(g)
        return h.ap()

    pA = psb("pA", [128, T], F32)
    pB = psb("pB", [128, T], F32)
    ptb = psb("ptb", [128, 1024], BF16)
    ptr = psb("ptr", [128, T], F32)
    pc32 = psb("pc32", [128, NF], F32)

    sems = {}
    for nm in ["qsy", "qw", "qxg", "qs", "qi", "qy", "SA", "SV", "SP"]:
        g = nc.semaphore(nm)
        h = g.__enter__()
        ctx_list.append(g)
        sems[nm] = h

    P = Prog()

    for (dst, src) in [(gcol, gcol_d), (bincol, bincol_d), (bdcol, bdcol_d),
                       (bocol, bocol_d), (cbcol, cbcol_d), (cwcol, cwcol_d),
                       (eyef, eyef_d), (eyeb, eyeb_d)]:
        P.op("sync", lambda e, dst=dst, src=src: e.dma_start(out=dst[:, :], in_=src[:, :]),
             inc="qsy", n=16)
    q_smalls = P.c("qsy")

    P.op("gpsimd", lambda e: e.dma_start(out=bcolb[:, :], in_=bcol_d[:, :]), inc="qw", n=16)
    q_bcolb = P.c("qw")
    for kk in range(KE):
        P.op("gpsimd",
             lambda e, kk=kk: e.dma_start(out=wd[:, kk, :],
                                          in_=wd_d[kk * 128:(kk + 1) * 128, :]),
             inc="qw", n=16)
    q_wd = P.c("qw")

    P.op("vector", lambda e: e.tensor_scalar_mul(nbdcol[:, :], bdcol[:, :], -1.0),
         waits=[("qsy", q_smalls)], inc="SV")
    P.op("vector", lambda e: e.tensor_copy(cin[:, :], bincol[:, :]), inc="SV")
    P.op("vector", lambda e: e.memset(vr[:, :, 0:2], 0.0), inc="SV")
    P.op("vector", lambda e: e.memset(fwd_c[:, :], 0.0), inc="SV")
    P.op("vector", lambda e: e.memset(bwd_c[:, :], 0.0), inc="SV")
    v_derived = P.c("SV")

    q_store_prev = 0
    v_cin_prev = 0
    for kk in range(KD):
        P.op("gpsimd",
             lambda e, kk=kk: e.dma_start(out=wstage[:, :],
                                          in_=win_d[kk * 128:(kk + 1) * 128, :]),
             waits=[("qsy", q_store_prev)], inc="qw", n=16)
        q_wst = P.c("qw")

        def cin_mms(e, kk=kk):
            last = None
            for f in range(NF):
                last = e.matmul(pc32[:, f:f + 1], wstage[:, f * 128:(f + 1) * 128],
                                bcolb[:, kk:kk + 1], start=True, stop=True)
            return last
        w_cin = [("qw", q_wst), ("qw", q_bcolb), ("SV", v_cin_prev)]
        P.op("tensor", cin_mms, waits=w_cin, inc="SP")
        p_cin = P.c("SP")
        P.op("vector", lambda e: e.tensor_add(cin[:, :], cin[:, :], pc32[:, :]),
             waits=[("SP", p_cin), ("SV", v_derived), ("SV", v_cin_prev)], inc="SV")
        v_cin_prev = P.c("SV")
        P.op("vector",
             lambda e, kk=kk: e.tensor_scalar_mul(wstage[:, :], wstage[:, :],
                                                  gcol[:, kk:kk + 1]),
             waits=[("SP", p_cin)], inc="SV")
        v_scaled = P.c("SV")
        P.op("sync",
             lambda e, kk=kk: e.dma_start(out=sc_win[:, :, kk, :],
                                          in_=wstage.rearrange("p (f c) -> p f c", f=NF)),
             waits=[("SV", v_scaled)], inc="qsy", n=16)
        q_store_prev = P.c("qsy")
    q_win_done = P.c("qsy")
    v_cin_done = v_cin_prev

    tr = {
        "xr_last_SA": [0] * 4, "xr_last_SV": [0] * 4,
        "winr_last_mm": [0] * 4, "bank_evac": {0: 0, 1: 0},
        "ptb_last_evac": 0, "ar_scan": [0] * 8, "last_hcm_reader": 0,
        "u_last_pe": [0, 0, 0, 0], "sneg_last_v": [0, 0], "cvt_last_silu": [0, 0],
        "gi_mm": 0, "v_hcm": {}, "v_cols": {}, "silu_all": {}, "wd_last": 0,
        "win_last": 0, "scan_last": {}, "conv_last": {},
    }
    a_spill, b_spill, f_spill, g_spill = [], [], [], []
    car_prev = {}

    c0, c1c, cm1, cE2, ctmp, cv1, cr1, cr2, crs2, cnm1, cA_, ccA = \
        0, 4, 8, 12, 16, 20, 24, 28, 32, 36, 40, 44

    def head_loads(k):
        t0 = k * T
        xq = []
        for s in range(NSUB):
            P.op("gpsimd",
                 lambda e, t0=t0, s=s: e.dma_start(
                     out=xr[:, s, :], in_=x_d[t0 + s * 128:t0 + (s + 1) * 128, :]),
                 waits=[("SA", tr["xr_last_SA"][s]), ("SV", tr["xr_last_SV"][s])],
                 inc="qxg", n=16)
            xq.append(P.c("qxg"))
        return xq

    def head_stats(k, xq):
        sq_m, red_m = [], []
        for s in range(NSUB):
            P.op("scalar",
                 lambda e, s=s: e.activation(out=xsq[:, :], in_=xr[:, s, :],
                                             func=AF.Square,
                                             accum_out=stA[:, c1c + s:c1c + s + 1]),
                 waits=[("qxg", xq[s])], inc="SA")
            sq_m.append(P.c("SA"))
            P.op("vector",
                 lambda e, s=s: e.tensor_reduce(stA[:, c0 + s:c0 + s + 1], xr[:, s, :],
                                                AX.X, ALU.add),
                 waits=[("qxg", xq[s])], inc="SV")
            red_m.append(P.c("SV"))
        return sq_m, red_m

    def head_colmath(k, sq_m, red_m):
        def ch(fn, extra=()):
            P.op("vector", fn, waits=[("SV", P.c("SV"))] + list(extra), inc="SV")
            return P.c("SV")
        ch(lambda e: e.tensor_scalar_mul(stA[:, cm1:cm1 + 4], stA[:, c0:c0 + 4], 1.0 / D),
           extra=[("SA", sq_m[-1]), ("SV", red_m[-1])])
        ch(lambda e: e.tensor_scalar_mul(stA[:, cE2:cE2 + 4], stA[:, c1c:c1c + 4], 1.0 / D))
        ch(lambda e: e.tensor_mul(stA[:, ctmp:ctmp + 4], stA[:, cm1:cm1 + 4], stA[:, cm1:cm1 + 4]))
        ch(lambda e: e.tensor_sub(stA[:, cv1:cv1 + 4], stA[:, cE2:cE2 + 4], stA[:, ctmp:ctmp + 4]))
        ch(lambda e: e.tensor_scalar_add(stA[:, ctmp:ctmp + 4], stA[:, cv1:cv1 + 4], EPS))
        ch(lambda e: e.reciprocal(stA[:, cr1:cr1 + 4], stA[:, ctmp:ctmp + 4]))
        ch(lambda e: e.tensor_mul(stA[:, ctmp:ctmp + 4], stA[:, cv1:cv1 + 4], stA[:, cr1:cr1 + 4]))
        ch(lambda e: e.tensor_scalar_add(stA[:, ctmp:ctmp + 4], stA[:, ctmp:ctmp + 4], EPS))
        v_recips = ch(lambda e: e.reciprocal(stA[:, cr2:cr2 + 4], stA[:, ctmp:ctmp + 4]))
        P.op("scalar",
             lambda e, k=k: e.activation(out=strs1[:, k * 4:(k + 1) * 4],
                                         in_=stA[:, cr1:cr1 + 4], func=AF.Sqrt),
             waits=[("SV", v_recips)])
        P.op("scalar", lambda e: e.activation(out=stA[:, crs2:crs2 + 4],
                                              in_=stA[:, cr2:cr2 + 4], func=AF.Sqrt),
             inc="SA")
        a_sqrts = P.c("SA")
        ch(lambda e: e.tensor_scalar_mul(stA[:, cnm1:cnm1 + 4], stA[:, cm1:cm1 + 4], -1.0),
           extra=[("SA", a_sqrts)])
        ch(lambda e, k=k: e.tensor_mul(stA[:, cA_:cA_ + 4], strs1[:, k * 4:(k + 1) * 4],
                                       stA[:, crs2:crs2 + 4]))
        ch(lambda e: e.tensor_mul(stA[:, ccA:ccA + 4], stA[:, cnm1:cnm1 + 4], stA[:, cA_:cA_ + 4]))
        tr["v_cols"][k] = ch(lambda e, k=k: e.tensor_mul(stc1[:, k * 4:(k + 1) * 4],
                                                         stA[:, cnm1:cnm1 + 4],
                                                         strs1[:, k * 4:(k + 1) * 4]))

    def head_apply(k, red_m):
        for s in range(NSUB):
            P.op("scalar",
                 lambda e, s=s: e.activation(out=ur[:, s, :], in_=xr[:, s, :],
                                             func=AF.Identity,
                                             scale=stA[:, cA_ + s:cA_ + s + 1],
                                             bias=stA[:, ccA + s:ccA + s + 1]),
                 waits=[("SV", tr["v_cols"][k]), ("SP", tr["u_last_pe"][s])], inc="SA")
            tr["xr_last_SA"][s] = P.c("SA")
            tr["xr_last_SV"][s] = red_m[s]

    def head_transposes(k):
        for s in range(NSUB):
            a_u = tr["xr_last_SA"][s]

            def htr(e, s=s):
                last = None
                for dd in range(KD):
                    last = e.transpose(ptb[:, dd * 128:(dd + 1) * 128],
                                       ur[:, s, dd * 128:(dd + 1) * 128], eyeb[:, :])
                return last
            P.op("tensor", htr, waits=[("SA", a_u), ("SV", tr["ptb_last_evac"])], inc="SP")
            p_tr = P.c("SP")
            tr["u_last_pe"][s] = p_tr
            P.op("vector",
                 lambda e, s=s: e.tensor_copy(h_cm[:, :, s * 128:(s + 1) * 128],
                                              ptb.rearrange("p (a b) -> p a b", a=KD)),
                 waits=[("SP", p_tr), ("SP", tr["last_hcm_reader"])], inc="SV")
            tr["ptb_last_evac"] = P.c("SV")
        tr["v_hcm"][k] = P.c("SV")

    def body_win(k):
        for f in range(NF):
            mglob = k * NF + f
            slot = mglob % 4
            P.op("sync",
                 lambda e, f=f, slot=slot: e.dma_start(out=win_r[:, slot, :, :],
                                                       in_=sc_win[:, f, :, :]),
                 waits=[("SP", tr["winr_last_mm"][slot]), ("qsy", q_win_done)],
                 inc="qsy", n=16)
            q_w = P.c("qsy")
            bank = tr["gi_mm"] % 2
            pps = pA if bank == 0 else pB

            def mmgrp(e, slot=slot, pps=pps):
                last = None
                for kk in range(KD):
                    last = e.matmul(pps[:, :], win_r[:, slot, kk, :], h_cm[:, kk, :],
                                    start=(kk == 0), stop=(kk == KD - 1))
                return last
            P.op("tensor", mmgrp,
                 waits=[("qsy", q_w), ("SV", tr["v_hcm"][k]),
                        ("SA", tr["bank_evac"][bank])], inc="SP")
            p_grp = P.c("SP")
            tr["gi_mm"] += 1
            tr["winr_last_mm"][slot] = p_grp
            if f == NF - 1:
                tr["last_hcm_reader"] = p_grp
                tr["win_last"] = p_grp

            if f < KE:
                ee = f
                w_ev = [("SP", p_grp)]
                if k > 0:
                    w_ev.append(("SV", car_prev[ee]))
                else:
                    w_ev.append(("SV", v_cin_done))
                P.op("scalar",
                     lambda e, ee=ee, pps=pps, f=f: e.activation(
                         out=vr[:, ee, 2:2 + T], in_=pps[:, :], func=AF.Identity,
                         bias=cin[:, f:f + 1]),
                     waits=w_ev, inc="SA")
                a_ev = P.c("SA")
                tr["bank_evac"][bank] = a_ev

                def conv(e, ee=ee):
                    e.tensor_scalar_mul(cvt[:, ee % 2, 0, :], vr[:, ee, 0:T],
                                        cwcol[:, 3 * ee:3 * ee + 1])
                    e.scalar_tensor_tensor(out=cvt[:, ee % 2, 1, :], in0=vr[:, ee, 1:1 + T],
                                           scalar=cwcol[:, 3 * ee + 1:3 * ee + 2],
                                           in1=cvt[:, ee % 2, 0, :],
                                           op0=ALU.mult, op1=ALU.add)
                    e.scalar_tensor_tensor(out=cvt[:, ee % 2, 0, :], in0=vr[:, ee, 2:2 + T],
                                           scalar=cwcol[:, 3 * ee + 2:3 * ee + 3],
                                           in1=cvt[:, ee % 2, 1, :],
                                           op0=ALU.mult, op1=ALU.add)
                    return e.tensor_copy(vr[:, ee, 0:2], vr[:, ee, T:T + 2])
                P.op("vector", conv,
                     waits=[("SA", a_ev), ("SA", tr["cvt_last_silu"][ee % 2])], inc="SV")
                car_prev[ee] = P.c("SV")
                tr["conv_last"][k] = P.c("SV")
                w_si = [("SV", car_prev[ee])]
                if k > 0:
                    w_si += [("SP", tr["vbf_SP"]), ("SV", tr["vbf_SV"])]
                P.op("scalar",
                     lambda e, ee=ee: e.activation(out=v_bf[:, ee, :],
                                                   in_=cvt[:, ee % 2, 0, :],
                                                   func=AF.Silu, bias=cbcol[:, ee:ee + 1]),
                     waits=w_si, inc="SA")
                tr["cvt_last_silu"][ee % 2] = P.c("SA")
                if ee == KE - 1:
                    tr["silu_all"][k] = P.c("SA")
            else:
                ee = f - KE
                grp = ee // 4
                s0 = (grp * 4) % 8
                slot8 = ee % 8
                g_glob = k * 4 + grp
                w_ev = [("SP", p_grp)]
                if g_glob >= 2:
                    w_ev.append(("qs", g_spill[g_glob - 2]))
                if k == 0:
                    w_ev.append(("SV", v_cin_done))
                P.op("scalar",
                     lambda e, pps=pps, f=f, slot8=slot8: e.activation(
                         out=sgr[:, slot8, :], in_=pps[:, :], func=AF.Sigmoid,
                         bias=cin[:, f:f + 1]),
                     waits=w_ev, inc="SA")
                tr["bank_evac"][bank] = P.c("SA")
                if ee % 4 == 3:
                    e0 = grp * 4
                    P.op("scalar",
                         lambda e, k=k, e0=e0, s0=s0: e.dma_start(
                             out=sc_g_v[:, e0:e0 + 4, k * T:(k + 1) * T],
                             in_=sgr[:, s0:s0 + 4, :]),
                         inc="qs", n=16)
                    g_spill.append(P.c("qs"))

    def body_wdelta(k):
        for ee in range(KE):
            bank = tr["gi_mm"] % 2
            pps = pA if bank == 0 else pB
            grp = ee // 4
            s0 = (grp * 4) % 8
            slot8 = ee % 8
            g_glob = k * 4 + grp

            def dmm(e, ee=ee, pps=pps):
                last = None
                for kk in range(KE):
                    last = e.matmul(pps[:, :], wd[:, kk, ee * 128:(ee + 1) * 128],
                                    v_bf[:, kk, :], start=(kk == 0), stop=(kk == KE - 1))
                return last
            w_mm = [("SA", tr["silu_all"][k]), ("SA", tr["bank_evac"][bank])]
            if k == 0 and ee == 0:
                w_mm.append(("qw", q_wd))
            P.op("tensor", dmm, waits=w_mm, inc="SP")
            p_grp = P.c("SP")
            tr["gi_mm"] += 1
            if ee == KE - 1:
                tr["wd_last"] = p_grp

            w_a = [("SP", p_grp), ("SV", tr["ar_scan"][slot8])]
            if g_glob >= 2:
                w_a.append(("qs", a_spill[g_glob - 2]))
            P.op("scalar",
                 lambda e, ee=ee, pps=pps, slot8=slot8: e.activation(
                     out=ar[:, slot8, :], in_=pps[:, :], func=AF.Sigmoid,
                     bias=bdcol[:, ee:ee + 1]),
                 waits=w_a, inc="SA")
            P.op("scalar",
                 lambda e, ee=ee, pps=pps: e.activation(
                     out=sneg[:, ee % 2, :], in_=pps[:, :], func=AF.Sigmoid,
                     scale=-1.0, bias=nbdcol[:, ee:ee + 1]),
                 waits=[("SV", tr["sneg_last_v"][ee % 2])], inc="SA")
            a_pair = P.c("SA")
            tr["bank_evac"][bank] = a_pair

            w_b = [("SA", a_pair)]
            if g_glob >= 2:
                w_b.append(("qs", b_spill[g_glob - 2]))
            P.op("vector",
                 lambda e, ee=ee, slot8=slot8: e.tensor_mul(
                     br[:, slot8, :], sneg[:, ee % 2, :], v_bf[:, ee, :]),
                 waits=w_b, inc="SV")
            tr["sneg_last_v"][ee % 2] = P.c("SV")

            w_s = []
            if g_glob >= 2:
                w_s.append(("qs", f_spill[g_glob - 2]))
            P.op("vector",
                 lambda e, ee=ee, slot8=slot8: e.tensor_tensor_scan(
                     out=fr[:, slot8, :], data0=ar[:, slot8, :], data1=br[:, slot8, :],
                     initial=fwd_c[:, ee:ee + 1], op0=ALU.mult, op1=ALU.add),
                 waits=w_s, inc="SV")
            v_scan0 = P.c("SV")
            P.op("vector",
                 lambda e, ee=ee, slot8=slot8: e.tensor_copy(fwd_c[:, ee:ee + 1],
                                                             fr[:, slot8, T - 1:T]),
                 waits=[("SV", v_scan0)], inc="SV")
            v_scan = P.c("SV")
            tr["ar_scan"][slot8] = v_scan
            tr["scan_last"][k] = v_scan

            if ee % 4 == 3:
                e0 = grp * 4
                P.op("scalar",
                     lambda e, k=k, e0=e0, s0=s0: e.dma_start(
                         out=sc_a_v[:, e0:e0 + 4, k * T:(k + 1) * T], in_=ar[:, s0:s0 + 4, :]),
                     waits=[("SV", v_scan)], inc="qs", n=16)
                a_spill.append(P.c("qs"))
                P.op("scalar",
                     lambda e, k=k, e0=e0, s0=s0: e.dma_start(
                         out=sc_b_v[:, e0:e0 + 4, k * T:(k + 1) * T], in_=br[:, s0:s0 + 4, :]),
                     inc="qs", n=16)
                b_spill.append(P.c(
                     "qs"))
                P.op("scalar",
                     lambda e, k=k, e0=e0, s0=s0: e.dma_start(
                         out=sc_f_v[:, e0:e0 + 4, k * T:(k + 1) * T], in_=fr[:, s0:s0 + 4, :]),
                     inc="qs", n=16)
                f_spill.append(P.c("qs"))
        tr["vbf_SP"] = P.c("SP")
        tr["vbf_SV"] = P.c("SV")

    xq0 = head_loads(0)
    sq0, red0 = head_stats(0, xq0)
    head_colmath(0, sq0, red0)
    head_apply(0, red0)
    head_transposes(0)
    for k in range(NCH):
        if k + 1 < NCH:
            xq_n = head_loads(k + 1)
        body_win(k)
        if k + 1 < NCH:
            sq_n, red_n = head_stats(k + 1, xq_n)
            head_colmath(k + 1, sq_n, red_n)
            head_apply(k + 1, red_n)
        body_wdelta(k)
        if k + 1 < NCH:
            head_transposes(k + 1)

    a_done = [("SP", P.c("SP")), ("SA", P.c("SA")), ("SV", P.c("SV")), ("qs", P.c("qs"))]

    for ee in range(KE):
        P.op("gpsimd",
             lambda e, ee=ee: e.dma_start(out=wout[:, ee, :],
                                          in_=wo_d[ee * 128:(ee + 1) * 128, :]),
             waits=[("SP", tr["wd_last"])] if ee == 0 else [], inc="qw", n=16)
    for kk in range(KD):
        P.op("gpsimd",
             lambda e, kk=kk: e.dma_start(out=weff[:, kk, :],
                                          in_=wft_d[kk * 128:(kk + 1) * 128, :]),
             inc="qw", n=16)
    P.op("gpsimd", lambda e: e.dma_start(out=dishab[:, :], in_=disha_d[:, :]), inc="qw", n=16)
    q_bw = P.c("qw")

    def weff_add(e):
        last = None
        for kk in range(KD):
            last = e.tensor_add(weff[:, kk, :], weff[:, kk, :], dishab[:, :])
        return last
    P.op("vector", weff_add, waits=[("SP", tr["wd_last"]), ("qw", q_bw)], inc="SV")
    v_weff = P.c("SV")

    if serial_b:
        P.serial = True
    if debug:
        dbg_fsall = nc.dram_tensor("dbg_fsall", [NCH * 128, KE * T], BF16,
                                   kind="ExternalOutput").ap()
        dbg_yall = nc.dram_tensor("dbg_yall", [NCH * 128, NSUB * D], F32,
                                  kind="ExternalOutput").ap()
    if debug:
        dbg_bwd = nc.dram_tensor("dbg_bwd", [NCH * 128, KE * T], BF16,
                                 kind="ExternalOutput").ap()
    dump_fs_q = 0
    dump_y_q = 0
    dump_bb_q = [0, 0]
    in_ring_ok = {"a": [0] * 8, "b": [0] * 8, "f": [0] * 8, "g": [0] * 8}
    xB_gate = (0, 0)
    yB_last_SA = 0
    x1_last_SA = 0
    aftm_attn_gate = 0
    ptr_last_evac_SA = 0
    aor_last_SP = [0, 0]
    lny_last_SV = [0, 0]
    l1r_last_SP = [0, 0]
    ln1cm_last_SP = 0
    fs_last_SP = 0
    outq = [0, 0, 0, 0]

    cb0, cb1, cbm, cbE, cbt, cbv, cbr = 0, 4, 8, 12, 16, 20, 24

    for ki in range(NCH):
        k = NCH - 1 - ki
        t0 = k * T
        q_in = {}

        def emit_group_load(grp, ki=ki, k=k):
            e0 = grp * 4
            s0 = e0 % 8
            for nm, sc_v, dst in [("a", sc_a_v, ain), ("b", sc_b_v, binb),
                                  ("f", sc_f_v, fwdin), ("g", sc_g_v, sgin)]:
                w = [("SV", max(in_ring_ok[nm][s0 + j] for j in range(4)))]
                if serial_loads:
                    w = w + [(s, v) for s, v in P.cnt.items() if v > 0]
                if ki == 0 and grp == 0:
                    w = a_done + w
                P.op("sync",
                     lambda e, sc_v=sc_v, dst=dst, e0=e0, s0=s0, k=k: e.dma_start(
                         out=dst[:, s0:s0 + 4, :],
                         in_=sc_v[:, e0:e0 + 4, k * T:(k + 1) * T]),
                     waits=w, inc="qi", n=16)
                q_in[(nm, grp)] = P.c("qi")
        emit_group_load(0)
        emit_group_load(1)
        wxl = [("SV", xB_gate[0]), ("qy", xB_gate[1])]
        if ki == 0:
            wxl = a_done + wxl
        P.op("sync",
             lambda e, t0=t0: e.dma_start(
                 out=xB[:, :, :],
                 in_=x_d[t0:t0 + T, :].rearrange("(a p) b -> p a b", p=128)),
             waits=wxl, inc="qi", n=16)
        q_x = P.c("qi")

        for ee in range(KE):
            grp = ee // 4
            s0 = ee % 8
            sl4 = ee % 4

            w_bs = [("qi", q_in[("a", grp)]), ("qi", q_in[("b", grp)])]
            if ki > 0 and ee == 0:
                w_bs.append(("SP", fs_last_SP))
                w_bs.append(("qsy", dump_fs_q))
            if debug:
                w_bs.append(("qsy", dump_bb_q[0]))
            P.op("vector",
                 lambda e, ee=ee, s0=s0, sl4=sl4: e.tensor_tensor_scan(
                     out=bb[:, sl4, :], data0=ain[:, s0, ::-1], data1=binb[:, s0, ::-1],
                     initial=bwd_c[:, ee:ee + 1], op0=ALU.mult, op1=ALU.add),
                 waits=w_bs, inc="SV")
            v_bs0 = P.c("SV")
            P.op("vector",
                 lambda e, ee=ee, sl4=sl4: e.tensor_copy(bwd_c[:, ee:ee + 1],
                                                         bb[:, sl4, T - 1:T]),
                 waits=[("SV", v_bs0)], inc="SV")
            v_b = P.c("SV")
            in_ring_ok["a"][s0] = v_b
            in_ring_ok["b"][s0] = v_b

            def fuse(e, ee=ee, s0=s0, sl4=sl4):
                e.tensor_add(tbb[:, :], fwdin[:, s0, :], bb[:, sl4, ::-1])
                return e.tensor_mul(fs_bf[:, ee, :], tbb[:, :], sgin[:, s0, :])
            P.op("vector", fuse,
                 waits=[("qi", q_in[("f", grp)]), ("qi", q_in[("g", grp)]),
                        ("SV", v_b)], inc="SV")
            v_f = P.c("SV")
            in_ring_ok["f"][s0] = v_f
            in_ring_ok["g"][s0] = v_f
            if debug and ee % 4 == 3:
                grp4 = ee // 4
                P.op("sync",
                     lambda e, k=k, grp4=grp4: e.dma_start(
                         out=dbg_bwd[k * 128:(k + 1) * 128,
                                     grp4 * 4 * T:(grp4 + 1) * 4 * T],
                         in_=bb.rearrange("p a b -> p (a b)")),
                     waits=[("SV", P.c("SV"))], inc="qsy", n=16)
                dump_bb_q[0] = P.c("qsy")
            if ee == 3:
                emit_group_load(2)
            elif ee == 7:
                emit_group_load(3)
        v_fused_all = P.c("SV")
        if debug:
            P.op("sync",
                 lambda e, k=k: e.dma_start(
                     out=dbg_fsall[k * 128:(k + 1) * 128, :],
                     in_=fs_bf.rearrange("p a b -> p (a b)")),
                 waits=[("SV", v_fused_all)], inc="qsy", n=16)
            dump_fs_q = P.c("qsy")

        for dd in range(KD):
            bank = tr["gi_mm"] % 2
            pps = pA if bank == 0 else pB

            def omm(e, dd=dd, pps=pps):
                last = None
                for kk in range(KE):
                    last = e.matmul(pps[:, :], wout[:, kk, dd * 128:(dd + 1) * 128],
                                    fs_bf[:, kk, :], start=(kk == 0), stop=(kk == KE - 1))
                return last
            w_mm = [("SV", v_fused_all), ("SA", tr["bank_evac"][bank])]
            if ki == 0 and dd == 0:
                w_mm.append(("qw", q_bw))
            P.op("tensor", omm, waits=w_mm, inc="SP")
            p_grp = P.c("SP")
            tr["gi_mm"] += 1
            P.op("scalar",
                 lambda e, dd=dd, pps=pps: e.activation(out=aor[:, dd % 2, :], in_=pps[:, :],
                                                        func=AF.Identity,
                                                        bias=bocol[:, dd:dd + 1]),
                 waits=[("SP", p_grp), ("SP", aor_last_SP[dd % 2])], inc="SA")
            a_ev = P.c("SA")
            tr["bank_evac"][bank] = a_ev

            def otr(e, dd=dd):
                last = None
                for s in range(NSUB):
                    last = e.transpose(ptr[:, s * 128:(s + 1) * 128],
                                       aor[:, dd % 2, s * 128:(s + 1) * 128], eyef[:, :])
                return last
            P.op("tensor", otr, waits=[("SA", a_ev), ("SA", ptr_last_evac_SA)], inc="SP")
            p_tr = P.c("SP")
            aor_last_SP[dd % 2] = p_tr

            def oev(e, dd=dd, k=k):
                last = None
                for s in range(NSUB):
                    last = e.activation(out=aftm[:, s, dd * 128:(dd + 1) * 128],
                                        in_=ptr[:, s * 128:(s + 1) * 128],
                                        func=AF.Identity,
                                        bias=stc1[:, k * 4 + s:k * 4 + s + 1])
                return last
            P.op("scalar", oev, waits=[("SP", p_tr)], inc="SA")
            ptr_last_evac_SA = P.c("SA")
        ev_attn = P.c("SA")

        yred, ysq = [], []
        for s in range(NSUB):
            P.op("vector",
                 lambda e, s=s, k=k: e.scalar_tensor_tensor(
                     out=yB[:, s, :], in0=xB[:, s, :],
                     scalar=strs1[:, k * 4 + s:k * 4 + s + 1], in1=aftm[:, s, :],
                     op0=ALU.mult, op1=ALU.add),
                 waits=[("SA", ev_attn), ("qi", q_x), ("SA", yB_last_SA),
                        ("qsy", dump_y_q)], inc="SV")
            v_y = P.c("SV")
            P.op("scalar",
                 lambda e, s=s: e.activation(out=xsq[:, :], in_=yB[:, s, :], func=AF.Square,
                                             accum_out=stB[:, cb1 + s:cb1 + s + 1]),
                 waits=[("SV", v_y)], inc="SA")
            ysq.append(P.c("SA"))
            P.op("vector",
                 lambda e, s=s: e.tensor_reduce(stB[:, cb0 + s:cb0 + s + 1], yB[:, s, :],
                                                AX.X, ALU.add), inc="SV")
            yred.append(P.c("SV"))

        def colmathB(rs_lo, cc_lo, red_last, sq_last):
            def chb(fn, extra=()):
                P.op("vector", fn, waits=[("SV", P.c("SV"))] + list(extra), inc="SV")
                return P.c("SV")
            chb(lambda e: e.tensor_scalar_mul(stB[:, cbm:cbm + 4], stB[:, cb0:cb0 + 4], 1.0 / D),
                extra=[("SV", red_last), ("SA", sq_last)])
            chb(lambda e: e.tensor_scalar_mul(stB[:, cbE:cbE + 4], stB[:, cb1:cb1 + 4], 1.0 / D))
            chb(lambda e: e.tensor_mul(stB[:, cbt:cbt + 4], stB[:, cbm:cbm + 4], stB[:, cbm:cbm + 4]))
            chb(lambda e: e.tensor_sub(stB[:, cbv:cbv + 4], stB[:, cbE:cbE + 4], stB[:, cbt:cbt + 4]))
            chb(lambda e: e.tensor_scalar_add(stB[:, cbt:cbt + 4], stB[:, cbv:cbv + 4], EPS))
            v_r = chb(lambda e: e.reciprocal(stB[:, cbr:cbr + 4], stB[:, cbt:cbt + 4]))
            P.op("scalar",
                 lambda e, rs_lo=rs_lo: e.activation(out=stB[:, rs_lo:rs_lo + 4],
                                                     in_=stB[:, cbr:cbr + 4], func=AF.Sqrt),
                 waits=[("SV", v_r)], inc="SA")
            a_rs = P.c("SA")
            chb(lambda e: e.tensor_scalar_mul(stB[:, cbt:cbt + 4], stB[:, cbm:cbm + 4], -1.0),
                extra=[("SA", a_rs)])
            return chb(lambda e, rs_lo=rs_lo, cc_lo=cc_lo: e.tensor_mul(
                stB[:, cc_lo:cc_lo + 4], stB[:, cbt:cbt + 4], stB[:, rs_lo:rs_lo + 4]))

        if debug:
            P.op("sync",
                 lambda e, k=k: e.dma_start(
                     out=dbg_yall[k * 128:(k + 1) * 128, :],
                     in_=yB.rearrange("p a b -> p (a b)")),
                 waits=[("SV", P.c("SV"))], inc="qsy", n=16)
            dump_y_q = P.c("qsy")
        v_ycols = colmathB(28, 36, yred[-1], ysq[-1])

        x1red, x1sq = [], []
        for s in range(NSUB):
            P.op("scalar",
                 lambda e, s=s: e.activation(out=lnyr[:, s % 2, :], in_=yB[:, s, :],
                                             func=AF.Identity,
                                             scale=stB[:, 28 + s:29 + s],
                                             bias=stB[:, 36 + s:37 + s]),
                 waits=[("SV", v_ycols), ("SV", lny_last_SV[s % 2])], inc="SA")
            a_l = P.c("SA")
            yB_last_SA = a_l
            P.op("vector",
                 lambda e, s=s: e.tensor_add(x1[:, s, :], lnyr[:, s % 2, :], xB[:, s, :]),
                 waits=[("SA", a_l), ("SA", x1_last_SA)], inc="SV")
            v_x1 = P.c("SV")
            lny_last_SV[s % 2] = v_x1
            P.op("scalar",
                 lambda e, s=s: e.activation(out=xsq[:, :], in_=x1[:, s, :], func=AF.Square,
                                             accum_out=stB[:, cb1 + s:cb1 + s + 1]),
                 waits=[("SV", v_x1)], inc="SA")
            x1sq.append(P.c("SA"))
            P.op("vector",
                 lambda e, s=s: e.tensor_reduce(stB[:, cb0 + s:cb0 + s + 1], x1[:, s, :],
                                                AX.X, ALU.add), inc="SV")
            x1red.append(P.c("SV"))
        v_x1cols = colmathB(40, 44, x1red[-1], x1sq[-1])

        for s in range(NSUB):
            P.op("scalar",
                 lambda e, s=s: e.activation(out=l1r[:, s % 2, :], in_=x1[:, s, :],
                                             func=AF.Identity,
                                             scale=stB[:, 40 + s:41 + s],
                                             bias=stB[:, 44 + s:45 + s]),
                 waits=[("SV", v_x1cols), ("SP", l1r_last_SP[s % 2])], inc="SA")
            a_l1 = P.c("SA")

            def l1tr(e, s=s):
                last = None
                for dd in range(KD):
                    last = e.transpose(ptb[:, dd * 128:(dd + 1) * 128],
                                       l1r[:, s % 2, dd * 128:(dd + 1) * 128], eyeb[:, :])
                return last
            P.op("tensor", l1tr, waits=[("SA", a_l1), ("SV", tr["ptb_last_evac"])], inc="SP")
            p_t = P.c("SP")
            l1r_last_SP[s % 2] = p_t
            P.op("vector",
                 lambda e, s=s: e.tensor_copy(ln1cm[:, :, s * 128:(s + 1) * 128],
                                              ptb.rearrange("p (a b) -> p a b", a=KD)),
                 waits=[("SP", p_t), ("SP", ln1cm_last_SP)], inc="SV")
            tr["ptb_last_evac"] = P.c("SV")
        v_ln1 = P.c("SV")

        for dd in range(KD):
            bank = tr["gi_mm"] % 2
            pps = pA if bank == 0 else pB

            def fmm(e, dd=dd, pps=pps):
                last = None
                for kk in range(KD):
                    last = e.matmul(pps[:, :], weff[:, kk, dd * 128:(dd + 1) * 128],
                                    ln1cm[:, kk, :], start=(kk == 0), stop=(kk == KD - 1))
                return last
            w_mm = [("SV", v_ln1), ("SA", tr["bank_evac"][bank])]
            if ki == 0 and dd == 0:
                w_mm.append(("SV", v_weff))
            P.op("tensor", fmm, waits=w_mm, inc="SP")
            p_grp = P.c("SP")
            tr["gi_mm"] += 1
            if dd == KD - 1:
                ln1cm_last_SP = p_grp
            P.op("scalar",
                 lambda e, dd=dd, pps=pps: e.copy(aor[:, dd % 2, :], pps[:, :]),
                 waits=[("SP", p_grp), ("SP", aor_last_SP[dd % 2])], inc="SA")
            a_ev = P.c("SA")
            tr["bank_evac"][bank] = a_ev

            def ftr(e, dd=dd):
                last = None
                for s in range(NSUB):
                    last = e.transpose(ptr[:, s * 128:(s + 1) * 128],
                                       aor[:, dd % 2, s * 128:(s + 1) * 128], eyef[:, :])
                return last
            P.op("tensor", ftr, waits=[("SA", a_ev), ("SA", ptr_last_evac_SA)], inc="SP")
            p_tr = P.c("SP")
            aor_last_SP[dd % 2] = p_tr

            def fev(e, dd=dd):
                last = None
                for s in range(NSUB):
                    last = e.copy(aftm[:, s, dd * 128:(dd + 1) * 128],
                                  ptr[:, s * 128:(s + 1) * 128])
                return last
            P.op("scalar", fev, waits=[("SP", p_tr)], inc="SA")
            ptr_last_evac_SA = P.c("SA")
        ev_f = P.c("SA")

        fred, fsq = [], []
        for s in range(NSUB):
            P.op("scalar",
                 lambda e, s=s: e.activation(out=xsq[:, :], in_=aftm[:, s, :], func=AF.Square,
                                             accum_out=stB[:, cb1 + s:cb1 + s + 1]),
                 waits=[("SA", ev_f)], inc="SA")
            fsq.append(P.c("SA"))
            P.op("vector",
                 lambda e, s=s: e.tensor_reduce(stB[:, cb0 + s:cb0 + s + 1], aftm[:, s, :],
                                                AX.X, ALU.add),
                 waits=[("SA", ev_f)], inc="SV")
            fred.append(P.c("SV"))
        v_fcols = colmathB(28, 36, fred[-1], fsq[-1])

        for s in range(NSUB):
            P.op("scalar",
                 lambda e, s=s: e.activation(out=lnyr[:, s % 2, :], in_=aftm[:, s, :],
                                             func=AF.Identity,
                                             scale=stB[:, 28 + s:29 + s],
                                             bias=stB[:, 36 + s:37 + s]),
                 waits=[("SV", v_fcols), ("SV", lny_last_SV[s % 2])], inc="SA")
            a_lf = P.c("SA")
            P.op("vector",
                 lambda e, s=s: e.tensor_add(xB[:, s, :], lnyr[:, s % 2, :], x1[:, s, :]),
                 waits=[("SA", a_lf), ("qy", outq[s])], inc="SV")
            v_o = P.c("SV")
            lny_last_SV[s % 2] = v_o
            P.op("scalar",
                 lambda e, s=s, t0=t0: e.dma_start(
                     out=y_d[t0 + s * 128:t0 + (s + 1) * 128, :], in_=xB[:, s, :]),
                 waits=[("SV", v_o)], inc="qy", n=16)
            outq[s] = P.c("qy")
        xB_gate = (P.c("SV"), outq[3])
        x1_last_SA = P.c("SA")
        fs_last_SP = P.c("SP")

    if debug:
        for nm_t, ap_t in [("dbg_yB", yB), ("dbg_x1", x1), ("dbg_aftm", aftm),
                           ("dbg_fs", fs_bf), ("dbg_ln1", ln1cm)]:
            d_t = nc.dram_tensor(nm_t, [128, ap_t.free_size()],
                                 ap_t.dtype, kind="ExternalOutput").ap()
            P.op("sync",
                 lambda e, d_t=d_t, ap_t=ap_t: e.dma_start(
                     out=d_t[:, :], in_=ap_t.rearrange(
                         "p a b -> p (a b)" if ap_t.ndim == 3 else "p a -> p a")),
                 waits=[("SA", P.c("SA")), ("SV", P.c("SV")), ("SP", P.c("SP"))],
                 inc="qsy", n=16)
        dbg_cin = nc.dram_tensor("dbg_cin", [128, NF], F32, kind="ExternalOutput").ap()
        dbg_rs1 = nc.dram_tensor("dbg_rs1", [128, NSUB * NCH], F32, kind="ExternalOutput").ap()
        dbg_c1 = nc.dram_tensor("dbg_c1", [128, NSUB * NCH], F32, kind="ExternalOutput").ap()
        wlast = [("SA", P.c("SA")), ("SV", P.c("SV"))]
        P.op("sync", lambda e: e.dma_start(out=dbg_cin[:, :], in_=cin[:, :]),
             waits=wlast, inc="qsy", n=16)
        P.op("sync", lambda e: e.dma_start(out=dbg_rs1[:, :], in_=strs1[:, :]), inc="qsy", n=16)
        P.op("sync", lambda e: e.dma_start(out=dbg_c1[:, :], in_=stc1[:, :]), inc="qsy", n=16)
    P.op("sync", lambda e: e.nop(),
         waits=[("qy", P.c("qy")), ("qs", P.c("qs")), ("qi", P.c("qi")), ("qsy", P.c("qsy"))])

    with nc.Block() as block:
        for eng_name in ("sync", "scalar", "vector", "tensor", "gpsimd"):
            def body(eng, eng_name=eng_name):
                done = {}
                for (e_nm, waits, fn, inc, n) in P.ops:
                    if e_nm != eng_name:
                        continue
                    for (snm, v) in waits:
                        if done.get(snm, 0) >= v:
                            continue
                        eng.wait_ge(sems[snm], v)
                        done[snm] = v
                    ins = fn(eng)
                    if inc is not None:
                        ins.then_inc(sems[inc], n)
            getattr(block, eng_name)(body)

    for g in reversed(ctx_list):
        g.__exit__(None, None, None)
    return nc


def make_in_maps(inputs, S):
    x = np.ascontiguousarray(np.asarray(inputs["L0_attn_x"], dtype=np.float32))
    g = np.asarray(inputs["L0_attn_ln_gamma"], dtype=np.float32)
    b = np.asarray(inputs["L0_attn_ln_beta"], dtype=np.float32)
    Win = np.ascontiguousarray(np.asarray(inputs["L0_attn_W_in"], dtype=np.float32))
    b_in = np.asarray(inputs["L0_attn_b_in"], dtype=np.float32)
    conv_w = np.asarray(inputs["L0_attn_conv_w"], dtype=np.float32)
    conv_b = np.asarray(inputs["L0_attn_conv_b"], dtype=np.float32)
    Wd = np.ascontiguousarray(np.asarray(inputs["L0_attn_W_delta"], dtype=np.float32))
    b_d = np.asarray(inputs["L0_attn_b_delta"], dtype=np.float32)
    Wo = np.ascontiguousarray(np.asarray(inputs["L0_attn_W_out"], dtype=np.float32))
    b_o = np.asarray(inputs["L0_attn_b_out"], dtype=np.float32)
    Wf = np.asarray(inputs["L0_ffn_weight"], dtype=np.float32)
    disha = np.ascontiguousarray(np.asarray(inputs["L0_ffn_disha"], dtype=np.float32))

    common = {
        "Win": Win,
        "Wd": Wd,
        "Wo": Wo,
        "WfT": np.ascontiguousarray(Wf.T),
        "disha": np.ascontiguousarray(np.concatenate([disha, disha], axis=0)),
        "gcol": np.ascontiguousarray(g.reshape(KD, 128).T),
        "bcol": np.ascontiguousarray(b.reshape(KD, 128).T),
        "bincol": np.ascontiguousarray(b_in.reshape(NF, 128).T),
        "bdcol": np.ascontiguousarray(b_d.reshape(KE, 128).T),
        "bocol": np.ascontiguousarray(b_o.reshape(KD, 128).T),
        "cbcol": np.ascontiguousarray(conv_b.reshape(KE, 128).T),
        "cwcol": np.ascontiguousarray(
            conv_w.reshape(KE, 128, 3).transpose(1, 0, 2).reshape(128, 3 * KE)),
        "eyef": np.eye(128, dtype=np.float32),
        "eyeb": np.eye(128, dtype=bfdt),
    }
    return [dict(common, x=np.ascontiguousarray(x[c, :S, :])) for c in range(8)]


S_FULL = 4096
_PROG_CACHE = {}


def kernel(**inputs):
    from concourse.bass_utils import run_bass_kernel_spmd

    if S_FULL not in _PROG_CACHE:
        _PROG_CACHE[S_FULL] = build_program(S_FULL)
    nc = _PROG_CACHE[S_FULL]
    in_maps = make_in_maps(inputs, S_FULL)
    res = run_bass_kernel_spmd(nc, in_maps, core_ids=list(range(8)))
    return np.stack([res.results[c]["y"] for c in range(8)], axis=0)


# version 04
# speedup vs baseline: 1.0696x; 1.0527x over previous
import numpy as np
import ml_dtypes
import concourse.bass as bass
import concourse.mybir as mybir

F32 = mybir.dt.float32
BF16 = mybir.dt.bfloat16
AF = mybir.ActivationFunctionType
ALU = mybir.AluOpType
AX = mybir.AxisListType

D = 1024
INNER = 2048
FF = 4096
KD = 8
KE = 16
NF = 32
T = 512
NSUB = 4
EPS = 1e-5
bfdt = ml_dtypes.bfloat16


class Prog:
    def __init__(self):
        self.ops = []
        self.cnt = {}
        self.serial = False

    def c(self, sem):
        return self.cnt.get(sem, 0)

    def op(self, eng, fn, waits=(), inc=None, n=1):
        ws = [(s, v) for (s, v) in waits if v and v > 0]
        if self.serial:
            ws = ws + [(s, v) for s, v in self.cnt.items() if v > 0]
        self.ops.append((eng, ws, fn, inc, n))
        if inc is not None:
            self.cnt[inc] = self.c(inc) + n
        return self.c(inc) if inc is not None else None


def carve(arena_ap, off, shape, dtype):
    n = 1
    for s in shape:
        n *= s
    if dtype == F32:
        sl = arena_ap[:, off:off + 2 * n].bitcast(F32)
    else:
        assert dtype == BF16
        sl = arena_ap[:, off:off + n]
    if len(shape) == 1:
        return sl
    pat = " ".join(f"a{i}" for i in range(len(shape)))
    kw = {f"a{i}": shape[i] for i in range(len(shape))}
    return sl.rearrange(f"p ({pat}) -> p {pat}", **kw)


def build_program(S, debug=False, serial_b=False, serial_loads=False):
    NCH = S // T
    nc = bass.Bass("TRN2", target_bir_lowering=False)
    sckind = "ExternalOutput" if debug else "Internal"

    x_d = nc.dram_tensor("x", [S, D], F32, kind="ExternalInput").ap()
    win_d = nc.dram_tensor("Win", [D, FF], F32, kind="ExternalInput").ap()
    wd_d = nc.dram_tensor("Wd", [INNER, INNER], F32, kind="ExternalInput").ap()
    wo_d = nc.dram_tensor("Wo", [INNER, D], F32, kind="ExternalInput").ap()
    wft_d = nc.dram_tensor("WfT", [D, D], F32, kind="ExternalInput").ap()
    disha_d = nc.dram_tensor("disha", [128, D], F32, kind="ExternalInput").ap()
    gcol_d = nc.dram_tensor("gcol", [128, KD], F32, kind="ExternalInput").ap()
    bcol_d = nc.dram_tensor("bcol", [128, KD], F32, kind="ExternalInput").ap()
    bincol_d = nc.dram_tensor("bincol", [128, NF], F32, kind="ExternalInput").ap()
    bdcol_d = nc.dram_tensor("bdcol", [128, KE], F32, kind="ExternalInput").ap()
    bocol_d = nc.dram_tensor("bocol", [128, KD], F32, kind="ExternalInput").ap()
    cbcol_d = nc.dram_tensor("cbcol", [128, KE], F32, kind="ExternalInput").ap()
    cwcol_d = nc.dram_tensor("cwcol", [128, 3 * KE], F32, kind="ExternalInput").ap()
    eyef_d = nc.dram_tensor("eyef", [128, 128], F32, kind="ExternalInput").ap()
    eyeb_d = nc.dram_tensor("eyeb", [128, 128], BF16, kind="ExternalInput").ap()
    y_d = nc.dram_tensor("y", [S, D], F32, kind="ExternalOutput").ap()

    sc_a = nc.dram_tensor("sc_a", [INNER, S], BF16, kind=sckind).ap()
    sc_b = nc.dram_tensor("sc_b", [INNER, S], BF16, kind=sckind).ap()
    sc_f = nc.dram_tensor("sc_f", [INNER, S], BF16, kind=sckind).ap()
    sc_g = nc.dram_tensor("sc_g", [INNER, S], BF16, kind=sckind).ap()
    sc_win = nc.dram_tensor("sc_win", [128, NF, KD, 128], BF16, kind="Internal").ap()

    sc_a_v = sc_a.rearrange("(e p) s -> p e s", p=128)
    sc_b_v = sc_b.rearrange("(e p) s -> p e s", p=128)
    sc_f_v = sc_f.rearrange("(e p) s -> p e s", p=128)
    sc_g_v = sc_g.rearrange("(e p) s -> p e s", p=128)

    ctx_list = []

    def sb(name, shape, dtype):
        g = nc.sbuf_tensor("s_" + name, shape, dtype)
        h = g.__enter__()
        ctx_list.append(g)
        return h.ap()

    gcol = sb("gcol", [128, KD], F32)
    bcolb = sb("bcolb", [128, KD], BF16)
    bincol = sb("bincol", [128, NF], F32)
    bdcol = sb("bdcol", [128, KE], F32)
    nbdcol = sb("nbdcol", [128, KE], F32)
    bocol = sb("bocol", [128, KD], F32)
    cbcol = sb("cbcol", [128, KE], F32)
    cwcol = sb("cwcol", [128, 3 * KE], F32)
    cin = sb("cin", [128, NF], F32)
    eyef = sb("eyef", [128, 128], F32)
    eyeb = sb("eyeb", [128, 128], BF16)
    dishab = sb("dishab", [128, D], BF16)
    strs1 = sb("strs1", [128, NSUB * NCH], F32)
    stc1 = sb("stc1", [128, NSUB * NCH], F32)
    stA = sb("stA", [128, 64], F32)
    stB = sb("stB", [128, 64], F32)
    fwd_c = sb("fwd_c", [128, KE], F32)
    bwd_c = sb("bwd_c", [128, KE], F32)
    xsq = sb("xsq", [128, D], F32)

    AR = 98304
    arena = sb("arena", [128, AR], BF16)

    wd = carve(arena, 0, [KE, INNER], BF16)
    win_r = carve(arena, 86080, [6, KD, 128], BF16)
    xr = carve(arena, 36864, [4, D], F32)
    h_cm = carve(arena, 45056, [KD, T], BF16)
    vr = carve(arena, 49152, [KE, T + 4], BF16)
    v_bf = carve(arena, 57408, [KE, T], BF16)
    ar = carve(arena, 65600, [8, T], BF16)
    br = carve(arena, 69696, [8, T], BF16)
    fr = carve(arena, 73792, [8, T], BF16)
    sgr = carve(arena, 77888, [8, T], BF16)
    cvt = carve(arena, 81984, [2, 2, T], F32)
    ur = carve(arena, 92224, [4, D], BF16)
    sneg = carve(arena, 32768, [2, T], F32)
    wstage = carve(arena, 65600, [2, FF], BF16)

    wout = carve(arena, 0, [KE, D], BF16)
    weff = carve(arena, 16384, [KD, D], BF16)
    ain = carve(arena, 24576, [4, T], BF16)
    binb = carve(arena, 26624, [4, T], BF16)
    fwdin = carve(arena, 28672, [8, T], BF16)
    sgin = carve(arena, 32768, [8, T], BF16)
    bb16 = carve(arena, 36864, [KE, T], BF16)
    fs_bf = carve(arena, 45056, [KE, T], BF16)
    xB = carve(arena, 53248, [NSUB, D], F32)
    yB = carve(arena, 61440, [NSUB, D], F32)
    x1 = carve(arena, 69632, [NSUB, D], F32)
    aftm = carve(arena, 77824, [NSUB, D], F32)
    ln1cm = carve(arena, 86016, [KD, T], BF16)
    lnyr = carve(arena, 90112, [2, D], F32)
    l1r = carve(arena, 94208, [2, D], BF16)
    aor = carve(arena, 96256, [2, T], F32)

    def psb(name, shape, dtype):
        g = nc.psum_tensor(name, shape, dtype)
        h = g.__enter__()
        ctx_list.append(g)
        return h.ap()

    pA = psb("pA", [128, T], F32)
    pB = psb("pB", [128, T], F32)
    pC = psb("pC", [128, T], F32)
    pD = psb("pD", [128, T], F32)
    pmm = [pA, pB, pC, pD]
    ptb = psb("ptb", [128, 1024], BF16)
    ptr = psb("ptr", [128, T], F32)
    pc32 = psb("pc32", [128, NF], F32)

    sems = {}
    for nm in ["qsy", "qw", "qxg", "qs", "qi", "qy", "SA", "SV", "SP"]:
        g = nc.semaphore(nm)
        h = g.__enter__()
        ctx_list.append(g)
        sems[nm] = h

    P = Prog()

    for (dst, src) in [(gcol, gcol_d), (bincol, bincol_d), (bdcol, bdcol_d),
                       (bocol, bocol_d), (cbcol, cbcol_d), (cwcol, cwcol_d),
                       (eyef, eyef_d), (eyeb, eyeb_d)]:
        P.op("sync", lambda e, dst=dst, src=src: e.dma_start(out=dst[:, :], in_=src[:, :]),
             inc="qsy", n=16)
    q_smalls = P.c("qsy")

    P.op("gpsimd", lambda e: e.dma_start(out=bcolb[:, :], in_=bcol_d[:, :]), inc="qw", n=16)
    q_bcolb = P.c("qw")
    for kk in range(KE):
        P.op("gpsimd",
             lambda e, kk=kk: e.dma_start(out=wd[:, kk, :],
                                          in_=wd_d[kk * 128:(kk + 1) * 128, :]),
             inc="qw", n=16)
    q_wd = P.c("qw")

    P.op("vector", lambda e: e.tensor_scalar_mul(nbdcol[:, :], bdcol[:, :], -1.0),
         waits=[("qsy", q_smalls)], inc="SV")
    P.op("vector", lambda e: e.tensor_copy(cin[:, :], bincol[:, :]), inc="SV")
    P.op("vector", lambda e: e.memset(vr[:, :, 0:2], 0.0), inc="SV")
    P.op("vector", lambda e: e.memset(fwd_c[:, :], 0.0), inc="SV")
    P.op("vector", lambda e: e.memset(bwd_c[:, :], 0.0), inc="SV")
    v_derived = P.c("SV")

    q_store = [q_smalls, q_smalls]
    v_cin_prev = 0
    for kk in range(KD):
        sl = kk % 2
        P.op("gpsimd",
             lambda e, kk=kk, sl=sl: e.dma_start(out=wstage[:, sl, :],
                                                 in_=win_d[kk * 128:(kk + 1) * 128, :]),
             waits=[("qsy", q_store[sl])], inc="qw", n=16)
        q_wst = P.c("qw")

        def cin_mms(e, kk=kk, sl=sl):
            last = None
            for f in range(NF):
                last = e.matmul(pc32[:, f:f + 1], wstage[:, sl, f * 128:(f + 1) * 128],
                                bcolb[:, kk:kk + 1], start=True, stop=True)
            return last
        w_cin = [("qw", q_wst), ("qw", q_bcolb), ("SV", v_cin_prev)]
        P.op("tensor", cin_mms, waits=w_cin, inc="SP")
        p_cin = P.c("SP")
        P.op("vector", lambda e: e.tensor_add(cin[:, :], cin[:, :], pc32[:, :]),
             waits=[("SP", p_cin), ("SV", v_derived), ("SV", v_cin_prev)], inc="SV")
        v_cin_prev = P.c("SV")
        P.op("vector",
             lambda e, kk=kk, sl=sl: e.tensor_scalar_mul(wstage[:, sl, :], wstage[:, sl, :],
                                                         gcol[:, kk:kk + 1]),
             waits=[("SP", p_cin)], inc="SV")
        v_scaled = P.c("SV")
        P.op("sync",
             lambda e, kk=kk, sl=sl: e.dma_start(
                 out=sc_win[:, :, kk, :],
                 in_=wstage[:, sl, :].rearrange("p (f c) -> p f c", f=NF)),
             waits=[("SV", v_scaled)], inc="qsy", n=16)
        q_store[sl] = P.c("qsy")
    q_win_done = P.c("qsy")
    v_cin_done = v_cin_prev

    tr = {
        "xr_last_SA": [0] * 4, "xr_last_SV": [0] * 4,
        "winr_last_mm": [0] * 6, "bank_evac": {0: 0, 1: 0, 2: 0, 3: 0},
        "ptb_last_evac": 0, "ar_scan": [0] * 8, "last_hcm_reader": 0,
        "u_last_pe": [0, 0, 0, 0], "sneg_last_v": [0, 0], "cvt_last_silu": [0, 0],
        "gi_mm": 0, "v_hcm": {}, "v_cols": {}, "silu_all": {}, "wd_last": 0,
        "win_last": 0, "scan_last": {}, "conv_last": {},
    }
    a_spill, b_spill, f_spill, g_spill = [], [], [], []
    car_prev = {}

    c0, c1c, cm1, cE2, ctmp, cv1, cr1, cr2, crs2, cnm1, cA_, ccA = \
        0, 4, 8, 12, 16, 20, 24, 28, 32, 36, 40, 44

    def head_loads(k):
        t0 = k * T
        xq = []
        for s in range(NSUB):
            P.op("gpsimd",
                 lambda e, t0=t0, s=s: e.dma_start(
                     out=xr[:, s, :], in_=x_d[t0 + s * 128:t0 + (s + 1) * 128, :]),
                 waits=[("SA", tr["xr_last_SA"][s]), ("SV", tr["xr_last_SV"][s])],
                 inc="qxg", n=16)
            xq.append(P.c("qxg"))
        return xq

    def head_stats(k, xq):
        sq_m, red_m = [], []
        for s in range(NSUB):
            P.op("scalar",
                 lambda e, s=s: e.activation(out=xsq[:, :], in_=xr[:, s, :],
                                             func=AF.Square,
                                             accum_out=stA[:, c1c + s:c1c + s + 1]),
                 waits=[("qxg", xq[s])], inc="SA")
            sq_m.append(P.c("SA"))
            P.op("vector",
                 lambda e, s=s: e.tensor_reduce(stA[:, c0 + s:c0 + s + 1], xr[:, s, :],
                                                AX.X, ALU.add),
                 waits=[("qxg", xq[s])], inc="SV")
            red_m.append(P.c("SV"))
        return sq_m, red_m

    def head_colmath(k, sq_m, red_m):
        def ch(fn, extra=()):
            P.op("vector", fn, waits=[("SV", P.c("SV"))] + list(extra), inc="SV")
            return P.c("SV")
        ch(lambda e: e.tensor_scalar_mul(stA[:, cm1:cm1 + 4], stA[:, c0:c0 + 4], 1.0 / D),
           extra=[("SA", sq_m[-1]), ("SV", red_m[-1])])
        ch(lambda e: e.tensor_scalar_mul(stA[:, cE2:cE2 + 4], stA[:, c1c:c1c + 4], 1.0 / D))
        ch(lambda e: e.tensor_mul(stA[:, ctmp:ctmp + 4], stA[:, cm1:cm1 + 4], stA[:, cm1:cm1 + 4]))
        ch(lambda e: e.tensor_sub(stA[:, cv1:cv1 + 4], stA[:, cE2:cE2 + 4], stA[:, ctmp:ctmp + 4]))
        ch(lambda e: e.tensor_scalar_add(stA[:, ctmp:ctmp + 4], stA[:, cv1:cv1 + 4], EPS))
        ch(lambda e: e.reciprocal(stA[:, cr1:cr1 + 4], stA[:, ctmp:ctmp + 4]))
        ch(lambda e: e.tensor_mul(stA[:, ctmp:ctmp + 4], stA[:, cv1:cv1 + 4], stA[:, cr1:cr1 + 4]))
        ch(lambda e: e.tensor_scalar_add(stA[:, ctmp:ctmp + 4], stA[:, ctmp:ctmp + 4], EPS))
        v_recips = ch(lambda e: e.reciprocal(stA[:, cr2:cr2 + 4], stA[:, ctmp:ctmp + 4]))
        P.op("scalar",
             lambda e, k=k: e.activation(out=strs1[:, k * 4:(k + 1) * 4],
                                         in_=stA[:, cr1:cr1 + 4], func=AF.Sqrt),
             waits=[("SV", v_recips)])
        P.op("scalar", lambda e: e.activation(out=stA[:, crs2:crs2 + 4],
                                              in_=stA[:, cr2:cr2 + 4], func=AF.Sqrt),
             inc="SA")
        a_sqrts = P.c("SA")
        ch(lambda e: e.tensor_scalar_mul(stA[:, cnm1:cnm1 + 4], stA[:, cm1:cm1 + 4], -1.0),
           extra=[("SA", a_sqrts)])
        ch(lambda e, k=k: e.tensor_mul(stA[:, cA_:cA_ + 4], strs1[:, k * 4:(k + 1) * 4],
                                       stA[:, crs2:crs2 + 4]))
        ch(lambda e: e.tensor_mul(stA[:, ccA:ccA + 4], stA[:, cnm1:cnm1 + 4], stA[:, cA_:cA_ + 4]))
        tr["v_cols"][k] = ch(lambda e, k=k: e.tensor_mul(stc1[:, k * 4:(k + 1) * 4],
                                                         stA[:, cnm1:cnm1 + 4],
                                                         strs1[:, k * 4:(k + 1) * 4]))

    def head_apply(k, red_m):
        for s in range(NSUB):
            P.op("scalar",
                 lambda e, s=s: e.activation(out=ur[:, s, :], in_=xr[:, s, :],
                                             func=AF.Identity,
                                             scale=stA[:, cA_ + s:cA_ + s + 1],
                                             bias=stA[:, ccA + s:ccA + s + 1]),
                 waits=[("SV", tr["v_cols"][k]), ("SP", tr["u_last_pe"][s])], inc="SA")
            tr["xr_last_SA"][s] = P.c("SA")
            tr["xr_last_SV"][s] = red_m[s]

    def head_transposes(k):
        for s in range(NSUB):
            a_u = tr["xr_last_SA"][s]

            def htr(e, s=s):
                last = None
                for dd in range(KD):
                    last = e.transpose(ptb[:, dd * 128:(dd + 1) * 128],
                                       ur[:, s, dd * 128:(dd + 1) * 128], eyeb[:, :])
                return last
            P.op("tensor", htr, waits=[("SA", a_u), ("SV", tr["ptb_last_evac"])], inc="SP")
            p_tr = P.c("SP")
            tr["u_last_pe"][s] = p_tr
            P.op("vector",
                 lambda e, s=s: e.tensor_copy(h_cm[:, :, s * 128:(s + 1) * 128],
                                              ptb.rearrange("p (a b) -> p a b", a=KD)),
                 waits=[("SP", p_tr), ("SP", tr["last_hcm_reader"])], inc="SV")
            tr["ptb_last_evac"] = P.c("SV")
        tr["v_hcm"][k] = P.c("SV")

    def body_win(k):
        for f in range(NF):
            mglob = k * NF + f
            slot = mglob % 6
            P.op("sync",
                 lambda e, f=f, slot=slot: e.dma_start(out=win_r[:, slot, :, :],
                                                       in_=sc_win[:, f, :, :]),
                 waits=[("SP", tr["winr_last_mm"][slot]), ("qsy", q_win_done)],
                 inc="qsy", n=16)
            q_w = P.c("qsy")
            bank = tr["gi_mm"] % 4
            pps = pmm[bank]

            def mmgrp(e, slot=slot, pps=pps):
                last = None
                for kk in range(KD):
                    last = e.matmul(pps[:, :], win_r[:, slot, kk, :], h_cm[:, kk, :],
                                    start=(kk == 0), stop=(kk == KD - 1))
                return last
            P.op("tensor", mmgrp,
                 waits=[("qsy", q_w), ("SV", tr["v_hcm"][k]),
                        ("SA", tr["bank_evac"][bank])], inc="SP")
            p_grp = P.c("SP")
            tr["gi_mm"] += 1
            tr["winr_last_mm"][slot] = p_grp
            if f == NF - 1:
                tr["last_hcm_reader"] = p_grp
                tr["win_last"] = p_grp

            if f < KE:
                ee = f
                w_ev = [("SP", p_grp)]
                if k > 0:
                    w_ev.append(("SV", car_prev[ee]))
                else:
                    w_ev.append(("SV", v_cin_done))
                P.op("scalar",
                     lambda e, ee=ee, pps=pps, f=f: e.activation(
                         out=vr[:, ee, 2:2 + T], in_=pps[:, :], func=AF.Identity,
                         bias=cin[:, f:f + 1]),
                     waits=w_ev, inc="SA")
                a_ev = P.c("SA")
                tr["bank_evac"][bank] = a_ev

                def conv(e, ee=ee):
                    e.tensor_scalar_mul(cvt[:, ee % 2, 0, :], vr[:, ee, 0:T],
                                        cwcol[:, 3 * ee:3 * ee + 1])
                    e.scalar_tensor_tensor(out=cvt[:, ee % 2, 1, :], in0=vr[:, ee, 1:1 + T],
                                           scalar=cwcol[:, 3 * ee + 1:3 * ee + 2],
                                           in1=cvt[:, ee % 2, 0, :],
                                           op0=ALU.mult, op1=ALU.add)
                    e.scalar_tensor_tensor(out=cvt[:, ee % 2, 0, :], in0=vr[:, ee, 2:2 + T],
                                           scalar=cwcol[:, 3 * ee + 2:3 * ee + 3],
                                           in1=cvt[:, ee % 2, 1, :],
                                           op0=ALU.mult, op1=ALU.add)
                    return e.tensor_copy(vr[:, ee, 0:2], vr[:, ee, T:T + 2])
                P.op("vector", conv,
                     waits=[("SA", a_ev), ("SA", tr["cvt_last_silu"][ee % 2])], inc="SV")
                car_prev[ee] = P.c("SV")
                tr["conv_last"][k] = P.c("SV")
                w_si = [("SV", car_prev[ee])]
                if k > 0:
                    w_si += [("SP", tr["vbf_SP"]), ("SV", tr["vbf_SV"])]
                P.op("scalar",
                     lambda e, ee=ee: e.activation(out=v_bf[:, ee, :],
                                                   in_=cvt[:, ee % 2, 0, :],
                                                   func=AF.Silu, bias=cbcol[:, ee:ee + 1]),
                     waits=w_si, inc="SA")
                tr["cvt_last_silu"][ee % 2] = P.c("SA")
                if ee == KE - 1:
                    tr["silu_all"][k] = P.c("SA")
            else:
                ee = f - KE
                grp = ee // 4
                s0 = (grp * 4) % 8
                slot8 = ee % 8
                g_glob = k * 4 + grp
                w_ev = [("SP", p_grp)]
                if g_glob >= 2:
                    w_ev.append(("qs", g_spill[g_glob - 2]))
                if k == 0:
                    w_ev.append(("SV", v_cin_done))
                P.op("scalar",
                     lambda e, pps=pps, f=f, slot8=slot8: e.activation(
                         out=sgr[:, slot8, :], in_=pps[:, :], func=AF.Sigmoid,
                         bias=cin[:, f:f + 1]),
                     waits=w_ev, inc="SA")
                a_sg = P.c("SA")
                tr["bank_evac"][bank] = a_sg
                if ee % 4 == 3:
                    e0 = grp * 4
                    P.op("scalar",
                         lambda e, k=k, e0=e0, s0=s0: e.dma_start(
                             out=sc_g_v[:, e0:e0 + 4, k * T:(k + 1) * T],
                             in_=sgr[:, s0:s0 + 4, :]),
                         waits=[("SA", a_sg)], inc="qs", n=16)
                    g_spill.append(P.c("qs"))

    def body_wdelta(k):
        for ee in range(KE):
            bank = tr["gi_mm"] % 4
            pps = pmm[bank]
            grp = ee // 4
            s0 = (grp * 4) % 8
            slot8 = ee % 8
            g_glob = k * 4 + grp

            def dmm(e, ee=ee, pps=pps):
                last = None
                for kk in range(KE):
                    last = e.matmul(pps[:, :], wd[:, kk, ee * 128:(ee + 1) * 128],
                                    v_bf[:, kk, :], start=(kk == 0), stop=(kk == KE - 1))
                return last
            w_mm = [("SA", tr["silu_all"][k]), ("SA", tr["bank_evac"][bank])]
            if k == 0 and ee == 0:
                w_mm.append(("qw", q_wd))
            P.op("tensor", dmm, waits=w_mm, inc="SP")
            p_grp = P.c("SP")
            tr["gi_mm"] += 1
            if ee == KE - 1:
                tr["wd_last"] = p_grp

            w_a = [("SP", p_grp), ("SV", tr["ar_scan"][slot8])]
            if g_glob >= 2:
                w_a.append(("qs", a_spill[g_glob - 2]))
            P.op("scalar",
                 lambda e, ee=ee, pps=pps, slot8=slot8: e.activation(
                     out=ar[:, slot8, :], in_=pps[:, :], func=AF.Sigmoid,
                     bias=bdcol[:, ee:ee + 1]),
                 waits=w_a, inc="SA")
            P.op("scalar",
                 lambda e, ee=ee, pps=pps: e.activation(
                     out=sneg[:, ee % 2, :], in_=pps[:, :], func=AF.Sigmoid,
                     scale=-1.0, bias=nbdcol[:, ee:ee + 1]),
                 waits=[("SV", tr["sneg_last_v"][ee % 2])], inc="SA")
            a_pair = P.c("SA")
            tr["bank_evac"][bank] = a_pair

            w_b = [("SA", a_pair)]
            if g_glob >= 2:
                w_b.append(("qs", b_spill[g_glob - 2]))
            P.op("vector",
                 lambda e, ee=ee, slot8=slot8: e.tensor_mul(
                     br[:, slot8, :], sneg[:, ee % 2, :], v_bf[:, ee, :]),
                 waits=w_b, inc="SV")
            tr["sneg_last_v"][ee % 2] = P.c("SV")

            w_s = []
            if g_glob >= 2:
                w_s.append(("qs", f_spill[g_glob - 2]))
            P.op("vector",
                 lambda e, ee=ee, slot8=slot8: e.tensor_tensor_scan(
                     out=fr[:, slot8, :], data0=ar[:, slot8, :], data1=br[:, slot8, :],
                     initial=fwd_c[:, ee:ee + 1], op0=ALU.mult, op1=ALU.add),
                 waits=w_s, inc="SV")
            v_scan0 = P.c("SV")
            P.op("vector",
                 lambda e, ee=ee, slot8=slot8: e.tensor_copy(fwd_c[:, ee:ee + 1],
                                                             fr[:, slot8, T - 1:T]),
                 waits=[("SV", v_scan0)], inc="SV")
            v_scan = P.c("SV")
            tr["ar_scan"][slot8] = v_scan
            tr["scan_last"][k] = v_scan

            if ee % 4 == 3:
                e0 = grp * 4
                P.op("scalar",
                     lambda e, k=k, e0=e0, s0=s0: e.dma_start(
                         out=sc_a_v[:, e0:e0 + 4, k * T:(k + 1) * T], in_=ar[:, s0:s0 + 4, :]),
                     waits=[("SV", v_scan)], inc="qs", n=16)
                a_spill.append(P.c("qs"))
                P.op("scalar",
                     lambda e, k=k, e0=e0, s0=s0: e.dma_start(
                         out=sc_b_v[:, e0:e0 + 4, k * T:(k + 1) * T], in_=br[:, s0:s0 + 4, :]),
                     inc="qs", n=16)
                b_spill.append(P.c(
                     "qs"))
                P.op("scalar",
                     lambda e, k=k, e0=e0, s0=s0: e.dma_start(
                         out=sc_f_v[:, e0:e0 + 4, k * T:(k + 1) * T], in_=fr[:, s0:s0 + 4, :]),
                     inc="qs", n=16)
                f_spill.append(P.c("qs"))
        tr["vbf_SP"] = P.c("SP")
        tr["vbf_SV"] = P.c("SV")

    xq0 = head_loads(0)
    sq0, red0 = head_stats(0, xq0)
    head_colmath(0, sq0, red0)
    head_apply(0, red0)
    head_transposes(0)
    for k in range(NCH):
        if k + 1 < NCH:
            xq_n = head_loads(k + 1)
        body_win(k)
        if k + 1 < NCH:
            sq_n, red_n = head_stats(k + 1, xq_n)
            head_colmath(k + 1, sq_n, red_n)
            head_apply(k + 1, red_n)
        body_wdelta(k)
        if k + 1 < NCH:
            head_transposes(k + 1)

    a_done = [("SP", P.c("SP")), ("SA", P.c("SA")), ("SV", P.c("SV")), ("qs", P.c("qs"))]

    for ee in range(KE):
        P.op("gpsimd",
             lambda e, ee=ee: e.dma_start(out=wout[:, ee, :],
                                          in_=wo_d[ee * 128:(ee + 1) * 128, :]),
             waits=[("SP", tr["wd_last"])] if ee == 0 else [], inc="qw", n=16)
    for kk in range(KD):
        P.op("gpsimd",
             lambda e, kk=kk: e.dma_start(out=weff[:, kk, :],
                                          in_=wft_d[kk * 128:(kk + 1) * 128, :]),
             inc="qw", n=16)
    P.op("gpsimd", lambda e: e.dma_start(out=dishab[:, :], in_=disha_d[:, :]), inc="qw", n=16)
    q_bw = P.c("qw")

    def weff_add(e):
        last = None
        for kk in range(KD):
            last = e.tensor_add(weff[:, kk, :], weff[:, kk, :], dishab[:, :])
        return last
    P.op("vector", weff_add, waits=[("SP", tr["wd_last"]), ("qw", q_bw)], inc="SV")
    v_weff = P.c("SV")

    if serial_b:
        P.serial = True
    qs_boundary = P.c("qs")
    in_fg_ok = {"f": [0, 0], "g": [0, 0]}
    ab_last_scan = 0
    q_ab = {}
    q_fg = {}
    bscan_done = {}
    fuse_done = {}
    wout_last = {}
    xB_gate = (0, 0)
    yB_last_SA = 0
    x1_last_SA = 0
    ptr_last_evac_SA = 0
    aor_last_SP = [0, 0]
    lny_last_SV = [0, 0]
    l1r_last_SP = [0, 0]
    ln1cm_last_SP = 0
    outq = [0, 0, 0, 0]

    cb0, cb1, cbm, cbE, cbt, cbv, cbr = 0, 4, 8, 12, 16, 20, 24

    def emit_ab_group(k, grp, first_chunk):
        nonlocal_dummy = None
        e0 = grp * 4
        w = [("SV", ab_last_scan)]
        if first_chunk:
            w += [("SP", tr["wd_last"]),
                  ("qs", a_spill[k * 4 + grp]), ("qs", b_spill[k * 4 + grp])]
        P.op("sync",
             lambda e, e0=e0, k=k: e.dma_start(
                 out=ain[:, :, :], in_=sc_a_v[:, e0:e0 + 4, k * T:(k + 1) * T]),
             waits=w, inc="qi", n=16)
        P.op("sync",
             lambda e, e0=e0, k=k: e.dma_start(
                 out=binb[:, :, :], in_=sc_b_v[:, e0:e0 + 4, k * T:(k + 1) * T]),
             inc="qi", n=16)
        q_ab[(k, grp)] = P.c("qi")

    def emit_bscans(k, grp):
        res = 0
        for j in range(4):
            ee = grp * 4 + j
            sl = ee % 4
            P.op("vector",
                 lambda e, ee=ee, sl=sl: e.tensor_tensor_scan(
                     out=bb16[:, ee, ::-1], data0=ain[:, sl, ::-1], data1=binb[:, sl, ::-1],
                     initial=bwd_c[:, ee:ee + 1], op0=ALU.mult, op1=ALU.add),
                 waits=[("qi", q_ab[(k, grp)])], inc="SV")
            v0 = P.c("SV")
            P.op("vector",
                 lambda e, ee=ee: e.tensor_copy(bwd_c[:, ee:ee + 1],
                                                bb16[:, ee, 0:1]),
                 waits=[("SV", v0)], inc="SV")
            res = P.c("SV")
        bscan_done[(k, grp)] = res
        return res

    def emit_fg_loads(k, grp, first_chunk):
        e0 = grp * 4
        s0 = e0 % 8
        half = (grp % 2)
        w = [("SV", in_fg_ok["f"][half])]
        if first_chunk:
            w += [("SP", tr["wd_last"]),
                  ("qs", f_spill[k * 4 + grp]), ("qs", g_spill[k * 4 + grp])]
        P.op("sync",
             lambda e, e0=e0, s0=s0, k=k: e.dma_start(
                 out=fwdin[:, s0:s0 + 4, :], in_=sc_f_v[:, e0:e0 + 4, k * T:(k + 1) * T]),
             waits=w, inc="qi", n=16)
        P.op("sync",
             lambda e, e0=e0, s0=s0, k=k: e.dma_start(
                 out=sgin[:, s0:s0 + 4, :], in_=sc_g_v[:, e0:e0 + 4, k * T:(k + 1) * T]),
             inc="qi", n=16)
        q_fg[(k, grp)] = P.c("qi")

    def colmathB(rs_lo, cc_lo, red_last, sq_last):
        def chb(fn, extra=()):
            P.op("vector", fn, waits=[("SV", P.c("SV"))] + list(extra), inc="SV")
            return P.c("SV")
        chb(lambda e: e.tensor_scalar_mul(stB[:, cbm:cbm + 4], stB[:, cb0:cb0 + 4], 1.0 / D),
            extra=[("SV", red_last), ("SA", sq_last)])
        chb(lambda e: e.tensor_scalar_mul(stB[:, cbE:cbE + 4], stB[:, cb1:cb1 + 4], 1.0 / D))
        chb(lambda e: e.tensor_mul(stB[:, cbt:cbt + 4], stB[:, cbm:cbm + 4], stB[:, cbm:cbm + 4]))
        chb(lambda e: e.tensor_sub(stB[:, cbv:cbv + 4], stB[:, cbE:cbE + 4], stB[:, cbt:cbt + 4]))
        chb(lambda e: e.tensor_scalar_add(stB[:, cbt:cbt + 4], stB[:, cbv:cbv + 4], EPS))
        v_r = chb(lambda e: e.reciprocal(stB[:, cbr:cbr + 4], stB[:, cbt:cbt + 4]))
        P.op("scalar",
             lambda e, rs_lo=rs_lo: e.activation(out=stB[:, rs_lo:rs_lo + 4],
                                                 in_=stB[:, cbr:cbr + 4], func=AF.Sqrt),
             waits=[("SV", v_r)], inc="SA")
        a_rs = P.c("SA")
        chb(lambda e: e.tensor_scalar_mul(stB[:, cbt:cbt + 4], stB[:, cbm:cbm + 4], -1.0),
            extra=[("SA", a_rs)])
        return chb(lambda e, rs_lo=rs_lo, cc_lo=cc_lo: e.tensor_mul(
            stB[:, cc_lo:cc_lo + 4], stB[:, cbt:cbt + 4], stB[:, rs_lo:rs_lo + 4]))

    k_first = NCH - 1
    for grp in range(4):
        emit_ab_group(k_first, grp, True)
        ab_last_scan = emit_bscans(k_first, grp)

    for ki in range(NCH):
        k = NCH - 1 - ki
        t0 = k * T
        first = (ki == 0)
        emit_fg_loads(k, 0, first)
        emit_fg_loads(k, 1, first)
        wxl = [("SV", xB_gate[0]), ("qy", xB_gate[1])]
        if first:
            wxl = a_done + wxl
        P.op("sync",
             lambda e, t0=t0: e.dma_start(
                 out=xB[:, :, :],
                 in_=x_d[t0:t0 + T, :].rearrange("(a p) b -> p a b", p=128)),
             waits=wxl, inc="qi", n=16)
        q_x = P.c("qi")

        for ee in range(KE):
            grp = ee // 4
            s0 = (grp * 4) % 8
            sl8 = ee % 8
            w_f = [("qi", q_fg[(k, grp)])]
            if ee == 0:
                w_f.append(("SV", bscan_done[(k, 3)]))
                if not first:
                    w_f.append(("SP", wout_last[k + 1]))
            def fuse(e, ee=ee, sl8=sl8):
                e.tensor_add(bb16[:, ee, :], fwdin[:, sl8, :], bb16[:, ee, :])
                return e.tensor_mul(fs_bf[:, ee, :], bb16[:, ee, :], sgin[:, sl8, :])
            P.op("vector", fuse, waits=w_f, inc="SV")
            half = grp % 2
            in_fg_ok["f"][half] = P.c("SV")
            if ee == 3:
                emit_fg_loads(k, 2, first)
            elif ee == 7:
                emit_fg_loads(k, 3, first)
        v_fused_all = P.c("SV")

        for dd in range(KD):
            bank = tr["gi_mm"] % 4
            pps = pmm[bank]

            def omm(e, dd=dd, pps=pps):
                last = None
                for kk in range(KE):
                    last = e.matmul(pps[:, :], wout[:, kk, dd * 128:(dd + 1) * 128],
                                    fs_bf[:, kk, :], start=(kk == 0), stop=(kk == KE - 1))
                return last
            w_mm = [("SV", v_fused_all), ("SA", tr["bank_evac"][bank])]
            if first and dd == 0:
                w_mm.append(("qw", q_bw))
                w_mm.append(("qs", qs_boundary))
            P.op("tensor", omm, waits=w_mm, inc="SP")
            p_grp = P.c("SP")
            tr["gi_mm"] += 1
            if dd == KD - 1:
                wout_last[k] = p_grp
            P.op("scalar",
                 lambda e, dd=dd, pps=pps: e.activation(out=aor[:, dd % 2, :], in_=pps[:, :],
                                                        func=AF.Identity,
                                                        bias=bocol[:, dd:dd + 1]),
                 waits=[("SP", p_grp), ("SP", aor_last_SP[dd % 2])], inc="SA")
            a_ev = P.c("SA")
            tr["bank_evac"][bank] = a_ev

            def otr(e, dd=dd):
                last = None
                for s in range(NSUB):
                    last = e.transpose(ptr[:, s * 128:(s + 1) * 128],
                                       aor[:, dd % 2, s * 128:(s + 1) * 128], eyef[:, :])
                return last
            P.op("tensor", otr, waits=[("SA", a_ev), ("SA", ptr_last_evac_SA)], inc="SP")
            p_tr = P.c("SP")
            aor_last_SP[dd % 2] = p_tr

            def oev(e, dd=dd, k=k):
                last = None
                for s in range(NSUB):
                    last = e.activation(out=aftm[:, s, dd * 128:(dd + 1) * 128],
                                        in_=ptr[:, s * 128:(s + 1) * 128],
                                        func=AF.Identity,
                                        bias=stc1[:, k * 4 + s:k * 4 + s + 1])
                return last
            P.op("scalar", oev, waits=[("SP", p_tr)], inc="SA")
            ptr_last_evac_SA = P.c("SA")
        ev_attn = P.c("SA")

        if k - 1 >= 0:
            emit_ab_group(k - 1, 0, False)
            ab_last_scan = emit_bscans(k - 1, 0)
            emit_ab_group(k - 1, 1, False)
            ab_last_scan = emit_bscans(k - 1, 1)

        yred, ysq = [], []
        for s in range(NSUB):
            P.op("vector",
                 lambda e, s=s, k=k: e.scalar_tensor_tensor(
                     out=yB[:, s, :], in0=xB[:, s, :],
                     scalar=strs1[:, k * 4 + s:k * 4 + s + 1], in1=aftm[:, s, :],
                     op0=ALU.mult, op1=ALU.add),
                 waits=[("SA", ev_attn), ("qi", q_x), ("SA", yB_last_SA)], inc="SV")
            v_y = P.c("SV")
            P.op("scalar",
                 lambda e, s=s: e.activation(out=xsq[:, :], in_=yB[:, s, :], func=AF.Square,
                                             accum_out=stB[:, cb1 + s:cb1 + s + 1]),
                 waits=[("SV", v_y)], inc="SA")
            ysq.append(P.c("SA"))
            P.op("vector",
                 lambda e, s=s: e.tensor_reduce(stB[:, cb0 + s:cb0 + s + 1], yB[:, s, :],
                                                AX.X, ALU.add), inc="SV")
            yred.append(P.c("SV"))
        v_ycols = colmathB(28, 36, yred[-1], ysq[-1])

        if k - 1 >= 0:
            emit_ab_group(k - 1, 2, False)
            ab_last_scan = emit_bscans(k - 1, 2)
            emit_ab_group(k - 1, 3, False)
            ab_last_scan = emit_bscans(k - 1, 3)

        x1red, x1sq = [], []
        for s in range(NSUB):
            P.op("scalar",
                 lambda e, s=s: e.activation(out=lnyr[:, s % 2, :], in_=yB[:, s, :],
                                             func=AF.Identity,
                                             scale=stB[:, 28 + s:29 + s],
                                             bias=stB[:, 36 + s:37 + s]),
                 waits=[("SV", v_ycols), ("SV", lny_last_SV[s % 2])], inc="SA")
            a_l = P.c("SA")
            yB_last_SA = a_l
            P.op("vector",
                 lambda e, s=s: e.tensor_add(x1[:, s, :], lnyr[:, s % 2, :], xB[:, s, :]),
                 waits=[("SA", a_l), ("SA", x1_last_SA)], inc="SV")
            v_x1 = P.c("SV")
            lny_last_SV[s % 2] = v_x1
            P.op("scalar",
                 lambda e, s=s: e.activation(out=xsq[:, :], in_=x1[:, s, :], func=AF.Square,
                                             accum_out=stB[:, cb1 + s:cb1 + s + 1]),
                 waits=[("SV", v_x1)], inc="SA")
            x1sq.append(P.c("SA"))
            P.op("vector",
                 lambda e, s=s: e.tensor_reduce(stB[:, cb0 + s:cb0 + s + 1], x1[:, s, :],
                                                AX.X, ALU.add), inc="SV")
            x1red.append(P.c("SV"))
        v_x1cols = colmathB(40, 44, x1red[-1], x1sq[-1])

        for s in range(NSUB):
            P.op("scalar",
                 lambda e, s=s: e.activation(out=l1r[:, s % 2, :], in_=x1[:, s, :],
                                             func=AF.Identity,
                                             scale=stB[:, 40 + s:41 + s],
                                             bias=stB[:, 44 + s:45 + s]),
                 waits=[("SV", v_x1cols), ("SP", l1r_last_SP[s % 2])], inc="SA")
            a_l1 = P.c("SA")

            def l1tr(e, s=s):
                last = None
                for dd in range(KD):
                    last = e.transpose(ptb[:, dd * 128:(dd + 1) * 128],
                                       l1r[:, s % 2, dd * 128:(dd + 1) * 128], eyeb[:, :])
                return last
            P.op("tensor", l1tr, waits=[("SA", a_l1), ("SV", tr["ptb_last_evac"])], inc="SP")
            p_t = P.c("SP")
            l1r_last_SP[s % 2] = p_t
            P.op("vector",
                 lambda e, s=s: e.tensor_copy(ln1cm[:, :, s * 128:(s + 1) * 128],
                                              ptb.rearrange("p (a b) -> p a b", a=KD)),
                 waits=[("SP", p_t), ("SP", ln1cm_last_SP)], inc="SV")
            tr["ptb_last_evac"] = P.c("SV")
        v_ln1 = P.c("SV")

        for dd in range(KD):
            bank = tr["gi_mm"] % 4
            pps = pmm[bank]

            def fmm(e, dd=dd, pps=pps):
                last = None
                for kk in range(KD):
                    last = e.matmul(pps[:, :], weff[:, kk, dd * 128:(dd + 1) * 128],
                                    ln1cm[:, kk, :], start=(kk == 0), stop=(kk == KD - 1))
                return last
            w_mm = [("SV", v_ln1), ("SA", tr["bank_evac"][bank])]
            if first and dd == 0:
                w_mm.append(("SV", v_weff))
            P.op("tensor", fmm, waits=w_mm, inc="SP")
            p_grp = P.c("SP")
            tr["gi_mm"] += 1
            if dd == KD - 1:
                ln1cm_last_SP = p_grp
            P.op("scalar",
                 lambda e, dd=dd, pps=pps: e.copy(aor[:, dd % 2, :], pps[:, :]),
                 waits=[("SP", p_grp), ("SP", aor_last_SP[dd % 2])], inc="SA")
            a_ev = P.c("SA")
            tr["bank_evac"][bank] = a_ev

            def ftr(e, dd=dd):
                last = None
                for s in range(NSUB):
                    last = e.transpose(ptr[:, s * 128:(s + 1) * 128],
                                       aor[:, dd % 2, s * 128:(s + 1) * 128], eyef[:, :])
                return last
            P.op("tensor", ftr, waits=[("SA", a_ev), ("SA", ptr_last_evac_SA)], inc="SP")
            p_tr = P.c("SP")
            aor_last_SP[dd % 2] = p_tr

            def fev(e, dd=dd):
                last = None
                for s in range(NSUB):
                    last = e.copy(aftm[:, s, dd * 128:(dd + 1) * 128],
                                  ptr[:, s * 128:(s + 1) * 128])
                return last
            P.op("scalar", fev, waits=[("SP", p_tr)], inc="SA")
            ptr_last_evac_SA = P.c("SA")
        ev_f = P.c("SA")

        fred, fsq = [], []
        for s in range(NSUB):
            P.op("scalar",
                 lambda e, s=s: e.activation(out=xsq[:, :], in_=aftm[:, s, :], func=AF.Square,
                                             accum_out=stB[:, cb1 + s:cb1 + s + 1]),
                 waits=[("SA", ev_f)], inc="SA")
            fsq.append(P.c("SA"))
            P.op("vector",
                 lambda e, s=s: e.tensor_reduce(stB[:, cb0 + s:cb0 + s + 1], aftm[:, s, :],
                                                AX.X, ALU.add),
                 waits=[("SA", ev_f)], inc="SV")
            fred.append(P.c("SV"))
        v_fcols = colmathB(28, 36, fred[-1], fsq[-1])

        for s in range(NSUB):
            P.op("scalar",
                 lambda e, s=s: e.activation(out=lnyr[:, s % 2, :], in_=aftm[:, s, :],
                                             func=AF.Identity,
                                             scale=stB[:, 28 + s:29 + s],
                                             bias=stB[:, 36 + s:37 + s]),
                 waits=[("SV", v_fcols), ("SV", lny_last_SV[s % 2])], inc="SA")
            a_lf = P.c("SA")
            P.op("vector",
                 lambda e, s=s: e.tensor_add(xB[:, s, :], lnyr[:, s % 2, :], x1[:, s, :]),
                 waits=[("SA", a_lf), ("qy", outq[s])], inc="SV")
            v_o = P.c("SV")
            lny_last_SV[s % 2] = v_o
            P.op("scalar",
                 lambda e, s=s, t0=t0: e.dma_start(
                     out=y_d[t0 + s * 128:t0 + (s + 1) * 128, :], in_=xB[:, s, :]),
                 waits=[("SV", v_o)], inc="qy", n=16)
            outq[s] = P.c("qy")
        xB_gate = (P.c("SV"), outq[3])
        x1_last_SA = P.c("SA")

    P.op("sync", lambda e: e.nop(),
         waits=[("qy", P.c("qy")), ("qs", P.c("qs")), ("qi", P.c("qi")), ("qsy", P.c("qsy"))])

    with nc.Block() as block:
        for eng_name in ("sync", "scalar", "vector", "tensor", "gpsimd"):
            def body(eng, eng_name=eng_name):
                done = {}
                for (e_nm, waits, fn, inc, n) in P.ops:
                    if e_nm != eng_name:
                        continue
                    for (snm, v) in waits:
                        if done.get(snm, 0) >= v:
                            continue
                        eng.wait_ge(sems[snm], v)
                        done[snm] = v
                    ins = fn(eng)
                    if inc is not None:
                        ins.then_inc(sems[inc], n)
            getattr(block, eng_name)(body)

    for g in reversed(ctx_list):
        g.__exit__(None, None, None)
    return nc


def make_in_maps(inputs, S):
    x = np.ascontiguousarray(np.asarray(inputs["L0_attn_x"], dtype=np.float32))
    g = np.asarray(inputs["L0_attn_ln_gamma"], dtype=np.float32)
    b = np.asarray(inputs["L0_attn_ln_beta"], dtype=np.float32)
    Win = np.ascontiguousarray(np.asarray(inputs["L0_attn_W_in"], dtype=np.float32))
    b_in = np.asarray(inputs["L0_attn_b_in"], dtype=np.float32)
    conv_w = np.asarray(inputs["L0_attn_conv_w"], dtype=np.float32)
    conv_b = np.asarray(inputs["L0_attn_conv_b"], dtype=np.float32)
    Wd = np.ascontiguousarray(np.asarray(inputs["L0_attn_W_delta"], dtype=np.float32))
    b_d = np.asarray(inputs["L0_attn_b_delta"], dtype=np.float32)
    Wo = np.ascontiguousarray(np.asarray(inputs["L0_attn_W_out"], dtype=np.float32))
    b_o = np.asarray(inputs["L0_attn_b_out"], dtype=np.float32)
    Wf = np.asarray(inputs["L0_ffn_weight"], dtype=np.float32)
    disha = np.ascontiguousarray(np.asarray(inputs["L0_ffn_disha"], dtype=np.float32))

    common = {
        "Win": Win,
        "Wd": Wd,
        "Wo": Wo,
        "WfT": np.ascontiguousarray(Wf.T),
        "disha": np.ascontiguousarray(np.concatenate([disha, disha], axis=0)),
        "gcol": np.ascontiguousarray(g.reshape(KD, 128).T),
        "bcol": np.ascontiguousarray(b.reshape(KD, 128).T),
        "bincol": np.ascontiguousarray(b_in.reshape(NF, 128).T),
        "bdcol": np.ascontiguousarray(b_d.reshape(KE, 128).T),
        "bocol": np.ascontiguousarray(b_o.reshape(KD, 128).T),
        "cbcol": np.ascontiguousarray(conv_b.reshape(KE, 128).T),
        "cwcol": np.ascontiguousarray(
            conv_w.reshape(KE, 128, 3).transpose(1, 0, 2).reshape(128, 3 * KE)),
        "eyef": np.eye(128, dtype=np.float32),
        "eyeb": np.eye(128, dtype=bfdt),
    }
    return [dict(common, x=np.ascontiguousarray(x[c, :S, :])) for c in range(8)]


S_FULL = 4096
_PROG_CACHE = {}


def kernel(**inputs):
    from concourse.bass_utils import run_bass_kernel_spmd

    if S_FULL not in _PROG_CACHE:
        _PROG_CACHE[S_FULL] = build_program(S_FULL)
    nc = _PROG_CACHE[S_FULL]
    in_maps = make_in_maps(inputs, S_FULL)
    res = run_bass_kernel_spmd(nc, in_maps, core_ids=list(range(8)))
    return np.stack([res.results[c]["y"] for c in range(8)], axis=0)
